# version 17
# baseline (speedup 1.0000x reference)
import contextlib
import re

import numpy as np

import concourse.bass as bass
import concourse.mybir as mybir
from concourse.bass_utils import run_bass_kernel_spmd

F32 = mybir.dt.float32
BF16 = mybir.dt.bfloat16
AF = mybir.ActivationFunctionType
ALU = mybir.AluOpType

B, T, D = 4, 8192, 1024
NH, HD = 16, 64
NCORES = 8
TC = T // 2
NCD = 8
TA = 512
NTA = TC // TA
TB = 256
NTB = TC // TB
DFF = 4096
NHC = DFF // 128
EPS_LN = 1e-5
EPS_RMS = float(np.finfo(np.float32).eps)
INV_D = 1.0 / D

RA_M, RA_E2, RA_T1, RA_T2 = 0, 512, 1024, 1536
RB_MSR, RB_REC, RB_RSTD = 0, 256, 512
RB_M2, RB_E2, RB_T1, RB_T2, RB_RSTD2 = 768, 1024, 1280, 1536, 1792


ENGS = ("sync", "gpsimd", "scalar", "tensor", "vector")


class _FakeInstr:
    __slots__ = ()

    def then_inc(self, sem, n):
        return self


class _FakeEng:
    __slots__ = ()
    _FI = _FakeInstr()

    def __getattr__(self, name):
        def f(*a, **k):
            return _FakeEng._FI
        return f


def build_nc():
    nc = bass.Bass("TRN2", target_bir_lowering=False, detect_race_conditions=False)

    xin = nc.dram_tensor("xin", [NTA, 128, NCD * TA], F32, kind="ExternalInput").ap()
    wrT = nc.dram_tensor("wrT", [D, D], F32, kind="ExternalInput").ap()
    wvT = nc.dram_tensor("wvT", [D, D], F32, kind="ExternalInput").ap()
    wwT = nc.dram_tensor("wwT", [D, D], F32, kind="ExternalInput").ap()
    woT = nc.dram_tensor("woT", [D, D], F32, kind="ExternalInput").ap()
    wkN = nc.dram_tensor("wkN", [D, D], F32, kind="ExternalInput").ap()
    w1n = nc.dram_tensor("w1n", [128, NHC * NCD * 128], F32, kind="ExternalInput").ap()
    w2 = nc.dram_tensor("w2", [DFF, D], F32, kind="ExternalInput").ap()
    b1r = nc.dram_tensor("b1r", [128, NHC], F32, kind="ExternalInput").ap()
    b2row = nc.dram_tensor("b2row", [1, D], F32, kind="ExternalInput").ap()
    emT = nc.dram_tensor("emT", [D, NH], F32, kind="ExternalInput").ap()
    emask = nc.dram_tensor("emask", [NH, D], F32, kind="ExternalInput").ap()
    onesf = nc.dram_tensor("onesf", [128, 128], F32, kind="ExternalInput").ap()
    selv = nc.dram_tensor("selv", [2, 1], F32, kind="ExternalInput").ap()

    yout = nc.dram_tensor("yout", [NTB, 128, NCD * TB], F32, kind="ExternalOutput").ap()

    scr_w = nc.dram_tensor("scr_w", [NTA, 128, NCD * TA], F32).ap()
    scr_u = nc.dram_tensor("scr_u", [NTA, 128, NCD * TA], F32).ap()
    scr_r = nc.dram_tensor("scr_r", [NTA, 128, NCD * TA], BF16).ap()
    scr_w2 = nc.dram_tensor("scr_w2", [NHC, 128, D], BF16).ap()
    cc_in = nc.dram_tensor("cc_in", [1, D], F32).ap()
    cc_out = nc.dram_tensor("cc_out", [2, D], F32).ap()

    xin_B = xin.rearrange("n p (c two t) -> n two p c t", two=2, t=TB)
    scr_w_B = scr_w.rearrange("n p (c two t) -> n two p c t", two=2, t=TB)
    scr_u_B = scr_u.rearrange("n p (c two t) -> n two p c t", two=2, t=TB)
    scr_r_B = scr_r.rearrange("n p (c two t) -> n two p c t", two=2, t=TB)
    scr_w2_G = scr_w2.rearrange("(g f) p e -> g p f e", f=4)
    wrT_L = wrT.rearrange("(k p) i -> p k i", p=128)
    wvT_L = wvT.rearrange("(k p) i -> p k i", p=128)
    wwT_L = wwT.rearrange("(k p) i -> p k i", p=128)
    woT_L = woT.rearrange("(k p) i -> p k i", p=128)
    wkN_L = wkN.rearrange("(k p) e -> p k e", p=128)
    emT_L = emT.rearrange("(k p) h -> p k h", p=128)

    es = contextlib.ExitStack()
    with es:
        ones_f = es.enter_context(nc.sbuf_tensor("ones_f", [128, 128], F32))
        ones_b = es.enter_context(nc.sbuf_tensor("ones_b", [128, 128], BF16))
        sel_s = es.enter_context(nc.sbuf_tensor("sel_s", [2, 1], F32))
        b1_s = es.enter_context(nc.sbuf_tensor("b1_s", [128, NHC], F32))
        b2b_s = es.enter_context(nc.sbuf_tensor("b2b_s", [1, D], BF16))
        ones_r = es.enter_context(nc.sbuf_tensor("ones_r", [1, TB], BF16))
        G_s = es.enter_context(nc.sbuf_tensor("G_s", [2, D], F32))
        carryA = es.enter_context(nc.sbuf_tensor("carryA", [128, NCD], F32))
        carryB = es.enter_context(nc.sbuf_tensor("carryB", [128, NCD], F32))
        rowA = es.enter_context(nc.sbuf_tensor("rowA", [1, 2048], F32))
        rowB = rowA


        PS1 = es.enter_context(nc.psum_tensor("PS1", [128, 512], F32))
        PS2 = es.enter_context(nc.psum_tensor("PS2", [128, 512], F32))
        PB1 = es.enter_context(nc.psum_tensor("PB1", [128, 512], F32))
        PB2 = es.enter_context(nc.psum_tensor("PB2", [128, 512], F32))
        PP = [es.enter_context(nc.psum_tensor(f"PP{k}", [128, 512], F32)) for k in range(2)]
        PK1 = es.enter_context(nc.psum_tensor("PK1", [128, 512], F32))
        PK2 = es.enter_context(nc.psum_tensor("PK2", [128, 512], F32))

        SEMN = ["tsem", "vsem", "ssem", "csem", "wt", "wtg", "wtw", "xb", "cc", "y",
                "wk0", "wk1", "w1s", "wos", "gss", "y0", "y1",
                "x0", "x1", "st0", "st1",
                "w2l0", "w2l1", "w2l2", "w2l3", "w2s0", "w2s1", "w2s2", "w2s3",
                "bx0", "bx1", "bx2", "bs0", "bs1", "g0", "g1", "g2"]
        SEM = {n: es.enter_context(nc.semaphore(n)) for n in SEMN}
        CNT = {n: 0 for n in SEMN}
        EV = {}
        PASS = [1]

        def inc(instr, s, n, evname=None):
            CNT[s] += n
            instr.then_inc(SEM[s], n)
            if evname is not None:
                if evname in EV and PASS[0] == 2:
                    assert EV[evname] == (s, CNT[s]), \
                        f"event {evname} diverged between passes"
                EV[evname] = (s, CNT[s])
            return instr

        def wt(eng, *names):
            for name in names:
                if name in EV:
                    s, v = EV[name]
                    eng.wait_ge(SEM[s], v)
                else:
                    assert name == "none" or re.search(r"-\d", name), \
                        f"wait on event {name!r} that was never recorded"

        _VOPS = ("tensor_tensor", "tensor_scalar", "tensor_copy", "reciprocal",
                 "tensor_tensor_scan", "memset", "scalar_tensor_tensor")

        class _VSer:
            __slots__ = ("_v",)

            def __init__(self, v):
                self._v = v

            def __getattr__(self, name):
                f = getattr(self._v, name)
                if name in _VOPS:
                    def wrapped(*a, **k):
                        self._v.wait_ge(SEM["vsem"], CNT["vsem"])
                        r = f(*a, **k)
                        CNT["vsem"] += 1
                        r.then_inc(SEM["vsem"], 1)
                        return r
                    return wrapped
                return f

        def vev(*names):
            for name in names:
                if name in EV and PASS[0] == 2:
                    assert EV[name] == ("vsem", CNT["vsem"]), \
                        f"event {name} diverged between passes"
                EV[name] = ("vsem", CNT["vsem"])

        PROG = []

        class _Sec:
            def __getattr__(self, eng_name):
                def reg(fn):
                    PROG.append((eng_name, fn))
                return reg

        block = _Sec()

        @block.sync
        def _(sync):
            sync.dma_start(out=ones_f[:, :], in_=onesf).then_inc(SEM["wt"], 16)
            sync.dma_start(out=sel_s[:, :], in_=selv).then_inc(SEM["wt"], 16)
            sync.dma_start(out=b1_s[:, :], in_=b1r).then_inc(SEM["wt"], 16)
            CNT["wt"] += 48
            EV["consts"] = ("wt", CNT["wt"])

        @block.gpsimd
        def _(g):
            g.dma_start(out=ones_b[:, :], in_=onesf).then_inc(SEM["wtg"], 16)
            g.dma_start(out=emask_s[:, :], in_=emask).then_inc(SEM["wtg"], 16)
            g.dma_start(out=emT_s[:, :].rearrange("p (k h) -> p k h", k=NCD),
                        in_=emT_L).then_inc(SEM["wtg"], 16)
            g.dma_start(out=b2b_s[:, :], in_=b2row).then_inc(SEM["wtg"], 16)
            CNT["wtg"] += 64
            EV["consts2"] = ("wtg", CNT["wtg"])

        @block.vector
        def _(v):
            v = _VSer(v)
            v.memset(carryA[:, :], 0.0)
            v.memset(ones_r[:, :], 1.0)
            vev("carry0")

        esA = contextlib.ExitStack()
        K_s = esA.enter_context(nc.sbuf_tensor("K_s", [16, TA], BF16))
        emask_s = esA.enter_context(nc.sbuf_tensor("emask_s", [16, D], BF16))
        emT_s = esA.enter_context(nc.sbuf_tensor("emT_s", [128, NCD * 16], BF16))
        wkhT_s = esA.enter_context(nc.sbuf_tensor("wkhT_s", [128, NCD * 16], BF16))
        WrT_s = esA.enter_context(nc.sbuf_tensor("WrT_s", [128, NCD * D], BF16))
        WvT_s = esA.enter_context(nc.sbuf_tensor("WvT_s", [128, NCD * D], BF16))
        WwT_s = esA.enter_context(nc.sbuf_tensor("WwT_s", [128, NCD * D], BF16))
        wkst = [esA.enter_context(nc.sbuf_tensor(f"wkst{k}", [128, D], BF16)) for k in range(2)]
        w2st = [esA.enter_context(nc.sbuf_tensor(f"w2st{k}", [128, D], BF16)) for k in range(4)]
        xA = [esA.enter_context(nc.sbuf_tensor(f"xA{k}", [128, NCD * TA], F32)) for k in range(2)]
        xbA = esA.enter_context(nc.sbuf_tensor("xbA", [128, NCD * TA], BF16))
        xsqA = esA.enter_context(nc.sbuf_tensor("xsqA", [128, NCD * TA], BF16))
        xnA = [esA.enter_context(nc.sbuf_tensor(f"xnA{k}", [128, NCD * TA], BF16)) for k in range(2)]
        sigw = esA.enter_context(nc.sbuf_tensor("sigw", [128, NCD * TA], F32))
        ubuf = esA.enter_context(nc.sbuf_tensor("ubuf", [128, NCD * TA], F32))
        vbf = esA.enter_context(nc.sbuf_tensor("vbf", [128, NCD * TA], BF16))
        rbf = esA.enter_context(nc.sbuf_tensor("rbf", [128, NCD * TA], BF16))
        sacr = esA.enter_context(nc.sbuf_tensor("sacr", [128, TA], F32))

        @block.gpsimd
        def _(g):
            g.dma_start(out=WrT_s[:, :].rearrange("p (k i) -> p k i", k=NCD),
                        in_=wrT_L).then_inc(SEM["wtw"], 16)
            g.dma_start(out=WvT_s[:, :].rearrange("p (k i) -> p k i", k=NCD),
                        in_=wvT_L).then_inc(SEM["wtw"], 16)
            g.dma_start(out=WwT_s[:, :].rearrange("p (k i) -> p k i", k=NCD),
                        in_=wwT_L).then_inc(SEM["wtw"], 16)
            CNT["wtw"] += 48
            EV["wts"] = ("wtw", CNT["wtw"])

        for ki in range(NCD):
            @block.gpsimd
            def _(g, ki=ki):
                wt(g, f"wkmm{ki-2}")
                sw = f"wk{ki % 2}"
                d = g.dma_start(out=wkst[ki % 2][:, :], in_=wkN_L[:, ki, :])
                d.then_inc(SEM[sw], 16)
                CNT[sw] += 16
                EV[f"wkld{ki}"] = (sw, CNT[sw])

            @block.tensor
            def _(t, ki=ki):
                if ki == 0:
                    wt(t, "consts2", "wts")
                wt(t, f"wkld{ki}")
                li = None
                for mi in range(NCD):
                    li = t.matmul(PK1[:, mi * 16:(mi + 1) * 16],
                                  wkst[ki % 2][:, mi * 128: mi * 128 + 128],
                                  emT_s[:, ki * 16:(ki + 1) * 16],
                                  start=(ki == 0 and mi == 0), stop=(ki == NCD - 1),
                                  skip_group_check=True)
                inc(li, "tsem", 1, f"wkmm{ki}")

        @block.vector
        def _(v):
            v = _VSer(v)
            wt(v, f"wkmm{NCD-1}")
            v.tensor_copy(wkhT_s[:, :], PK1[:, 0:128])
            vev("wkh")


        RQ = {"q": 0}
        qevent = {}

        def a_xload(i):
            if i >= NTA:
                return

            @block.sync
            def _(sync, i=i):
                wt(sync, f"xn{i-2}", f"xsq{i-2}")
                d = sync.dma_start(out=xA[i % 2][:, :], in_=xin[i])
                d.then_inc(SEM[f"x{i % 2}"], 16)
                CNT[f"x{i % 2}"] += 16
                EV[f"xA{i}"] = (f"x{i % 2}", CNT[f"x{i % 2}"])

        def a_xbload(i):
            if i >= NTA:
                return

            @block.gpsimd
            def _(g, i=i):
                wt(g, f"stats{i-1}")
                d = g.dma_start(out=xbA[:, :], in_=xin[i])
                d.then_inc(SEM["xb"], 16)
                CNT["xb"] += 16
                EV[f"xb{i}"] = ("xb", CNT["xb"])

        def a_xsq_half(i, h):
            if i >= NTA:
                return

            @block.vector
            def _(v, i=i, h=h):
                v = _VSer(v)
                if h == 0:
                    wt(v, f"xA{i}", f"stats{i-1}")
                sl = slice(h * 4 * TA, (h + 1) * 4 * TA)
                v.tensor_tensor(xsqA[:, sl], xA[i % 2][:, sl], xA[i % 2][:, sl],
                                op=ALU.mult)
                if h == 1:
                    vev(f"xsq{i}")

        def a_stats(i):
            if i >= NTA:
                return

            @block.tensor
            def _(t, i=i):
                wt(t, f"xb{i}", f"xsq{i}", f"rows{i-1}")
                for ki in range(NCD):
                    t.matmul(PS1[0:1, 0:TA], ones_b[:, 0:1], xbA[:, ki * TA:(ki + 1) * TA],
                             start=(ki == 0), stop=(ki == NCD - 1))
                li = None
                for ki in range(NCD):
                    li = t.matmul(PS2[0:1, 0:TA], ones_b[:, 0:1], xsqA[:, ki * TA:(ki + 1) * TA],
                                  start=(ki == 0), stop=(ki == NCD - 1))
                inc(li, "tsem", 1, f"stats{i}")

        def a_rows(i):
            if i >= NTA:
                return

            @block.vector
            def _(v, i=i):
                v = _VSer(v)
                wt(v, f"stats{i}", f"sqrt{i-1}", f"bcast{i-1}")
                v.tensor_scalar(rowA[0:1, RA_M:RA_M + TA], PS1[0:1, 0:TA], INV_D, None,
                                op0=ALU.mult)
                v.tensor_scalar(rowA[0:1, RA_E2:RA_E2 + TA], PS2[0:1, 0:TA], INV_D, EPS_LN,
                                op0=ALU.mult, op1=ALU.add)
                v.tensor_tensor(rowA[0:1, RA_T1:RA_T1 + TA], rowA[0:1, RA_M:RA_M + TA],
                                rowA[0:1, RA_M:RA_M + TA], op=ALU.mult)
                v.tensor_tensor(rowA[0:1, RA_T2:RA_T2 + TA], rowA[0:1, RA_E2:RA_E2 + TA],
                                rowA[0:1, RA_T1:RA_T1 + TA], op=ALU.subtract)
                v.reciprocal(rowA[0:1, RA_T1:RA_T1 + TA], rowA[0:1, RA_T2:RA_T2 + TA])
                vev(f"rows{i}")

        def a_sqrt(i):
            if i >= NTA:
                return

            @block.scalar
            def _(sc, i=i):
                wt(sc, f"rows{i}")
                inc(sc.activation(rowA[0:1, RA_T2:RA_T2 + TA], rowA[0:1, RA_T1:RA_T1 + TA],
                                  AF.Sqrt), "ssem", 1, f"sqrt{i}")

        def a_bcast(i):
            if i >= NTA:
                return

            @block.tensor
            def _(t, i=i):
                wt(t, "consts", f"sqrt{i}", f"xn{i-1}")
                t.matmul(PB1[:, 0:TA], ones_f[0:1, :], rowA[0:1, RA_M:RA_M + TA],
                         start=True, stop=True)
                li = t.matmul(PB2[:, 0:TA], ones_f[0:1, :], rowA[0:1, RA_T2:RA_T2 + TA],
                              start=True, stop=True)
                inc(li, "tsem", 1, f"bcast{i}")

        def a_xn_half(i, h):
            if i >= NTA:
                return

            @block.vector
            def _(v, i=i, h=h):
                v = _VSer(v)
                if h == 0:
                    wt(v, f"bcast{i}", f"xA{i}", f"pr{i-2}m{NCD-1}")
                for c in range(4 * h, 4 * h + 4):
                    sl = slice(c * TA, (c + 1) * TA)
                    v.tensor_tensor(xA[i % 2][:, sl], xA[i % 2][:, sl], PB1[:, 0:TA],
                                    op=ALU.subtract)
                    v.tensor_tensor(xnA[i % 2][:, sl], xA[i % 2][:, sl], PB2[:, 0:TA],
                                    op=ALU.mult)
                if h == 1:
                    vev(f"xn{i}")

        def a_xn(i):
            a_xn_half(i, 0)
            a_xn_half(i, 1)

        a_xload(0)
        a_xload(1)
        a_xbload(0)
        a_xsq_half(0, 0)
        a_xsq_half(0, 1)
        a_stats(0)
        a_xbload(1)
        a_rows(0)
        a_sqrt(0)
        a_bcast(0)
        a_xn(0)

        def t_kb(t, i, cc):
            wt(t, f"Kev{i}", f"u{i}c{cc-1}" if cc > 0 else f"u{i-1}c{NCD-1}")
            inc(t.matmul(PK2[:, 0:TA], emask_s[:, cc * 128:(cc + 1) * 128],
                         K_s[:, :], start=True, stop=True),
                "tsem", 1, f"kb{i}c{cc}")

        def b_uscan(i, c):
            @block.vector
            def _(v, i=i, c=c):
                v = _VSer(v)
                sl = slice(c * TA, (c + 1) * TA)
                wt(v, f"vb{i}c{c}", f"kb{i}c{c}", f"stu{i-1}")
                v.tensor_tensor(ubuf[:, sl], vbf[:, sl], PK2[:, 0:TA], op=ALU.mult)
                vev(f"u{i}c{c}")
                if c == 0:
                    wt(v, "carry0")
                v.tensor_tensor_scan(sacr[:, :], sigw[:, sl], ubuf[:, sl],
                                     carryA[:, c:c + 1], op0=ALU.mult, op1=ALU.add)
                v.tensor_copy(carryA[:, c:c + 1], sacr[:, TA - 1: TA])
                if c == NCD - 1:
                    vev(f"scan{i}")

        ROTW = [PP[0], PP[1], PK1]

        def emit_r(i, deferred=False):
            sst = f"st{i % 2}"
            for mi in range(NCD):
                bank = ROTW[RQ["q"] % 3]
                wait_prev = qevent.get(RQ["q"] - 3, "none")
                qevent[RQ["q"]] = f"r{i}m{mi}"
                RQ["q"] += 1

                @block.tensor
                def _(t, i=i, mi=mi, bank=bank, wait_prev=wait_prev):
                    wt(t, wait_prev)
                    li = None
                    for ki in range(NCD):
                        li = t.matmul(bank[:, 0:TA],
                                      WrT_s[:, ki * D + mi * 128: ki * D + mi * 128 + 128],
                                      xnA[i % 2][:, ki * TA:(ki + 1) * TA],
                                      start=(ki == 0), stop=(ki == NCD - 1))
                    inc(li, "tsem", 1, f"pr{i}m{mi}")

                @block.scalar
                def _(sc, i=i, mi=mi, bank=bank):
                    wt(sc, f"pr{i}m{mi}", f"str{i-1}")
                    inc(sc.activation(rbf[:, mi * TA:(mi + 1) * TA],
                                      bank[:, 0:TA], AF.Sigmoid),
                        "ssem", 1, f"r{i}m{mi}")

                if deferred:
                    @block.scalar
                    def _(sc, i=i, mi=mi, sst=sst):
                        wt(sc, f"r{i}m{mi}")
                        d = sc.dma_start(out=scr_r[i][:, mi * TA:(mi + 1) * TA],
                                         in_=rbf[:, mi * TA:(mi + 1) * TA])
                        d.then_inc(SEM[sst], 16)
                        CNT[sst] += 16
                        if mi == NCD - 1:
                            EV[f"str{i}"] = (sst, CNT[sst])

        for i in range(NTA):
            par = i % 2
            sst = f"st{par}"

            a_xload(i + 2)

            @block.gpsimd
            def _(g, i=i):
                for k in range(4):
                    s = i * 4 + k
                    sp = s % 4
                    wt(g, f"w2s{s-4}")
                    d = g.dma_start(out=w2st[s % 4][:, :], in_=w2[s * 128:(s + 1) * 128, :])
                    d.then_inc(SEM[f"w2l{sp}"], 16)
                    CNT[f"w2l{sp}"] += 16
                    EV[f"w2l{s}"] = (f"w2l{sp}", CNT[f"w2l{sp}"])

            @block.sync
            def _(sync, i=i):
                for k in range(4):
                    s = i * 4 + k
                    sp = s % 4
                    wt(sync, f"w2l{s}")
                    d = sync.dma_start(out=scr_w2[s], in_=w2st[s % 4][:, :])
                    d.then_inc(SEM[f"w2s{sp}"], 16)
                    CNT[f"w2s{sp}"] += 16
                    EV[f"w2s{s}"] = (f"w2s{sp}", CNT[f"w2s{sp}"])

            @block.tensor
            def _(t, i=i):
                wt(t, "wkh", f"Kev{i-1}", f"u{i-1}c{NCD-1}", f"xn{i}")
                li = None
                for ki in range(NCD):
                    li = t.matmul(PK2[0:16, 0:TA], wkhT_s[:, ki * 16:(ki + 1) * 16],
                                  xnA[i % 2][:, ki * TA:(ki + 1) * TA],
                                  start=(ki == 0), stop=(ki == NCD - 1))
                inc(li, "tsem", 1, f"K{i}")

            @block.scalar
            def _(sc, i=i):
                wt(sc, f"K{i}")
                inc(sc.copy(K_s[:, :], PK2[0:16, 0:TA]), "ssem", 1, f"Kev{i}")

            for c in range(NCD):
                bank = ROTW[RQ["q"] % 3]
                wait_prev = qevent.get(RQ["q"] - 3, "none")
                qevent[RQ["q"]] = f"sgw{i}m{c}"
                RQ["q"] += 1

                @block.tensor
                def _(t, i=i, c=c, bank=bank, wait_prev=wait_prev):
                    if c == 0:
                        wt(t, f"xn{i}")
                    wt(t, wait_prev)
                    if bank is PK1:
                        wt(t, "none")
                    li = None
                    for ki in range(NCD):
                        li = t.matmul(bank[:, 0:TA],
                                      WwT_s[:, ki * D + c * 128: ki * D + c * 128 + 128],
                                      xnA[i % 2][:, ki * TA:(ki + 1) * TA],
                                      start=(ki == 0), stop=(ki == NCD - 1))
                    inc(li, "tsem", 1, f"pw{i}m{c}")

                @block.scalar
                def _(sc, i=i, c=c, bank=bank):
                    wt(sc, f"pw{i}m{c}", f"stw{i-1}", f"scan{i-1}")
                    inc(sc.activation(sigw[:, c * TA:(c + 1) * TA],
                                      bank[:, 0:TA], AF.Sigmoid),
                        "ssem", 1, f"sgw{i}m{c}")

                @block.vector
                def _(v, i=i, c=c):
                    v = _VSer(v)
                    wt(v, f"sgw{i}m{c}")
                    v.tensor_scalar(sigw[:, c * TA:(c + 1) * TA],
                                    sigw[:, c * TA:(c + 1) * TA], 0.99, None,
                                    op0=ALU.mult)
                    vev(f"wpc{i}m{c}")

                bank2 = ROTW[RQ["q"] % 3]
                wait_prev2 = qevent.get(RQ["q"] - 3, "none")
                qevent[RQ["q"]] = f"vb{i}c{c}"
                RQ["q"] += 1

                @block.tensor
                def _(t, i=i, c=c, bank2=bank2, wait_prev2=wait_prev2):
                    wt(t, wait_prev2)
                    li = None
                    for ki in range(NCD):
                        li = t.matmul(bank2[:, 0:TA],
                                      WvT_s[:, ki * D + c * 128: ki * D + c * 128 + 128],
                                      xnA[i % 2][:, ki * TA:(ki + 1) * TA],
                                      start=(ki == 0), stop=(ki == NCD - 1))
                    inc(li, "tsem", 1, f"pv{i}c{c}")
                    t_kb(t, i, c)

                @block.scalar
                def _(sc, i=i, c=c, bank2=bank2):
                    wt(sc, f"pv{i}c{c}", f"u{i-1}c{c}")
                    inc(sc.copy(vbf[:, c * TA:(c + 1) * TA], bank2[:, 0:TA]),
                        "ssem", 1, f"vb{i}c{c}")

                b_uscan(i, c)

                if c == 3:
                    a_xsq_half(i + 1, 0)
                if c == 5:
                    a_xsq_half(i + 1, 1)

            a_stats(i + 1)
            a_rows(i + 1)
            a_xbload(i + 2)

            if i < NTA - 2:
                emit_r(i)

            a_sqrt(i + 1)
            a_bcast(i + 1)
            a_xn_half(i + 1, 0)
            a_xn_half(i + 1, 1)

            @block.sync
            def _(sync, i=i, sst=sst):
                wt(sync, f"wpc{i}m{NCD-1}")
                sync.dma_start(out=scr_w[i], in_=sigw[:, :]).then_inc(SEM[sst], 16)
                CNT[sst] += 16
                wt(sync, f"u{i}c{NCD-1}")
                sync.dma_start(out=scr_u[i], in_=ubuf[:, :]).then_inc(SEM[sst], 16)
                CNT[sst] += 16
                EV[f"stw{i}"] = (sst, CNT[sst])
                EV[f"stu{i}"] = (sst, CNT[sst])
                if i < NTA - 2:
                    wt(sync, f"r{i}m{NCD-1}")
                    sync.dma_start(out=scr_r[i], in_=rbf[:, :]).then_inc(SEM[sst], 16)
                    CNT[sst] += 16
                    EV[f"str{i}"] = (sst, CNT[sst])

        @block.sync
        def _(sync):
            wt(sync, f"scan{NTA-1}")
            for c in range(NCD):
                d = sync.dma_start(out=cc_in[0, c * 128:(c + 1) * 128], in_=carryA[:, c:c + 1])
                d.then_inc(SEM["cc"], 16)
            CNT["cc"] += 16 * NCD
            EV["carry"] = ("cc", CNT["cc"])

        @block.gpsimd
        def _(g):
            wt(g, "carry")
            cci = g.collective_compute(
                "AllGather", ALU.bypass,
                replica_groups=[[2 * p, 2 * p + 1] for p in range(NCORES // 2)],
                ins=[cc_in], outs=[cc_out],
            )
            cci.then_inc(SEM["csem"], 1)
            CNT["csem"] += 1
            EV["ccdone"] = ("csem", CNT["csem"])

        emit_r(NTA - 2, deferred=True)
        emit_r(NTA - 1, deferred=True)

        A_EARLY = [f"pr{NTA-1}m{NCD-1}", f"scan{NTA-1}",
                   f"sgw{NTA-1}m{NCD-1}", f"xn{NTA-1}", f"xn{NTA-2}",
                   f"u{NTA-1}c{NCD-1}", f"stats{NTA-1}", f"xsq{NTA-1}",
                   f"stw{NTA-1}", f"stw{NTA-2}",
                   f"w2s{4*NTA-1}", f"w2s{4*NTA-2}", f"w2s{4*NTA-3}", f"w2s{4*NTA-4}",
                   f"xA{NTA-1}", f"xA{NTA-2}", f"xb{NTA-1}",
                   f"wkld{NCD-1}", f"wkld{NCD-2}"]
        A_DONE = A_EARLY + [f"r{NTA-1}m{NCD-1}", f"str{NTA-1}", f"str{NTA-2}"]

        esA.close()

        w1_s = es.enter_context(nc.sbuf_tensor("w1_s", [128, NCD * DFF], BF16))
        woT_s = es.enter_context(nc.sbuf_tensor("woT_s", [128, NCD * D], BF16))
        w2g = [es.enter_context(nc.sbuf_tensor(f"w2g{k}", [128, 4 * D], BF16)) for k in range(3)]
        xB = [es.enter_context(nc.sbuf_tensor(f"xB{k}", [128, NCD * TB], F32)) for k in range(3)]
        wpB = [es.enter_context(nc.sbuf_tensor(f"wpB{k}", [128, NCD * TB], F32)) for k in range(2)]
        uB = es.enter_context(nc.sbuf_tensor("uB", [128, NCD * TB], F32))
        rB = [es.enter_context(nc.sbuf_tensor(f"rB{k}", [128, NCD * TB], BF16)) for k in range(2)]
        sB = es.enter_context(nc.sbuf_tensor("sB", [128, TB], F32))
        sh = [es.enter_context(nc.sbuf_tensor(f"sh{k}", [128, NCD * TB], BF16)) for k in range(2)]
        h_s = es.enter_context(nc.sbuf_tensor("h_s", [128, NHC * TB], BF16))
        rstdB_s = es.enter_context(nc.sbuf_tensor("rstdB_s", [128, TB], F32))
        m2B_s = es.enter_context(nc.sbuf_tensor("m2B_s", [128, TB], F32))
        rstd2B_s = es.enter_context(nc.sbuf_tensor("rstd2B_s", [128, TB], F32))

        @block.gpsimd
        def _(g):
            wt(g, *A_EARLY)
            g.dma_start(out=woT_s[:, :].rearrange("p (k i) -> p k i", k=NCD),
                        in_=woT_L).then_inc(SEM["wos"], 16)
            CNT["wos"] += 16
            EV["wo"] = ("wos", CNT["wos"])

            for gidx in range(3):
                d = g.dma_start(out=w2g[gidx][:, :].rearrange("p (f e) -> p f e", f=4),
                                in_=scr_w2_G[gidx])
                d.then_inc(SEM[f"g{gidx}"], 16)
                CNT[f"g{gidx}"] += 16
                EV[f"w2g{gidx}"] = (f"g{gidx}", CNT[f"g{gidx}"])

            wt(g, f"r{NTA-1}m{NCD-1}", f"str{NTA-1}", f"str{NTA-2}")
            for k in range(8):
                d = g.dma_start(out=w1_s[:, k * 4096:(k + 1) * 4096],
                                in_=w1n[:, k * 4096:(k + 1) * 4096])
                d.then_inc(SEM["w1s"], 16)
                CNT["w1s"] += 16
                EV[f"w1c{k}"] = ("w1s", CNT["w1s"])


        def b_loads_x(j):
            if j >= NTB:
                return

            @block.sync
            def _(sync, j=j):
                p3 = j % 3
                if j == 0:
                    wt(sync, *A_EARLY)
                wt(sync, f"y{j-3}")
                sbx = f"bx{p3}"
                d = sync.dma_start(out=xB[p3][:, :].rearrange("p (c t) -> p c t", t=TB),
                                   in_=xin_B[j // 2, j % 2])
                d.then_inc(SEM[sbx], 16)
                CNT[sbx] += 16
                EV[f"Bxr{j}"] = (sbx, CNT[sbx])

        def b_loads_wu(j):
            if j >= NTB:
                return

            def issue(eng, j, with_u):
                par = j % 2
                sbs = f"bs{par}"
                eng.dma_start(out=wpB[par][:, :].rearrange("p (c t) -> p c t", t=TB),
                              in_=scr_w_B[j // 2, j % 2]).then_inc(SEM[sbs], 16)
                CNT[sbs] += 16
                EV[f"Bwuw{j}"] = (sbs, CNT[sbs])
                if with_u:
                    issue_u(eng, j)
                d = eng.dma_start(out=rB[par][:, :].rearrange("p (c t) -> p c t", t=TB),
                                  in_=scr_r_B[j // 2, j % 2])
                d.then_inc(SEM[sbs], 16)
                CNT[sbs] += 16
                EV[f"Bwur{j}"] = (sbs, CNT[sbs])

            def issue_u(eng, j):
                wt(eng, f"bscan{j-1}")
                sbs = f"bs{j % 2}"
                eng.dma_start(out=uB[:, :].rearrange("p (c t) -> p c t", t=TB),
                              in_=scr_u_B[j // 2, j % 2]).then_inc(SEM[sbs], 16)
                CNT[sbs] += 16
                EV[f"Bwuu{j}"] = (sbs, CNT[sbs])

            if j < 2:
                @block.sync
                def _(sync, j=j):
                    wt(sync, *A_EARLY)
                    wt(sync, f"pr{NTA-1}m{NCD-1}")
                    issue(sync, j, with_u=(j == 0))
            else:
                @block.scalar
                def _(sc, j=j):
                    wt(sc, f"batn{j-2}", f"bxn2{j-2}", f"bs2{j-2}")
                    issue(sc, j, with_u=True)

            if j == 1:
                LATE_U.append(issue_u)

        def b_scan(j):
            @block.vector
            def _(v, j=j):
                v = _VSer(v)
                par = j % 2
                if j == 0:
                    wt(v, "cB")
                wt(v, f"Bwuw{j}", f"Bwuu{j}")
                for c in range(NCD):
                    sl = slice(c * TB, (c + 1) * TB)
                    v.tensor_tensor_scan(sB[:, :], wpB[par][:, sl], uB[:, sl],
                                         carryB[:, c:c + 1], op0=ALU.mult, op1=ALU.add)
                    v.tensor_copy(carryB[:, c:c + 1], sB[:, TB - 1:TB])
                    if c == NCD - 1:
                        vev(f"bscan{j}")
                    if c == 0:
                        wt(v, f"Bwur{j}")
                    v.tensor_tensor(wpB[par][:, sl], sB[:, :], rB[par][:, sl], op=ALU.mult)
                vev(f"bout{j}")

        def b_sq(j):
            @block.scalar
            def _(sc, j=j):
                par = j % 2
                wt(sc, f"bout{j}", f"f1{j-2}h{NHC-1}")
                inc(sc.activation(sh[par][:, :], wpB[par][:, :], AF.Square),
                    "ssem", 1, f"bsq{j}")

        def t_brs(t, j):
            par = j % 2
            wt(t, f"bsq{j}", f"brow2{j-1}")
            li = None
            for ki in range(NCD):
                li = t.matmul(PS1[0:1, 0:TB], ones_b[:, 0:1], sh[par][:, ki * TB:(ki + 1) * TB],
                              start=(ki == 0), stop=(ki == NCD - 1))
            inc(li, "tsem", 1, f"brs{j}")

        def b_rows(j):
            @block.vector
            def _(v, j=j):
                v = _VSer(v)
                wt(v, f"brs{j}", f"bsqr{j-1}")
                v.tensor_scalar(rowB[0:1, RB_MSR:RB_MSR + TB], PS1[0:1, 0:TB], INV_D, EPS_RMS,
                                op0=ALU.mult, op1=ALU.add)
                v.reciprocal(rowB[0:1, RB_REC:RB_REC + TB], rowB[0:1, RB_MSR:RB_MSR + TB])
                vev(f"brow{j}")

        def b_sqrt(j):
            @block.scalar
            def _(sc, j=j):
                wt(sc, f"brow{j}", f"bbc{j-1}")
                inc(sc.activation(rowB[0:1, RB_RSTD:RB_RSTD + TB],
                                  rowB[0:1, RB_REC:RB_REC + TB], AF.Sqrt),
                    "ssem", 1, f"bsqr{j}")

        def t_bbc(t, j):
            wt(t, f"bsqr{j}", f"bevac2{j-1}")
            inc(t.matmul(PS2[:, 0:TB], ones_f[0:1, :], rowB[0:1, RB_RSTD:RB_RSTD + TB],
                         start=True, stop=True), "tsem", 1, f"bbc{j}")

        def b_evacr(j):
            @block.vector
            def _(v, j=j):
                v = _VSer(v)
                par = j % 2
                wt(v, f"bbc{j}")
                v.tensor_copy(rstdB_s[:, :], PS2[:, 0:TB])
                vev(f"bevacr{j}")

        def b_attn(j):
            @block.vector
            def _(v, j=j):
                v = _VSer(v)
                par = j % 2
                wt(v, f"bevacr{j}", f"brs{j}")
                for c in range(NCD):
                    sl = slice(c * TB, (c + 1) * TB)
                    v.tensor_tensor(sh[par][:, sl], wpB[par][:, sl],
                                    rstdB_s[:, :], op=ALU.mult)
                vev(f"batn{j}")

        def t_wo(t, j, mi):
            par = j % 2
            if mi == 0:
                wt(t, "wo", f"batn{j}")
            if j == 0 and mi < 2:
                wt(t, f"r{NTA-1}m{NCD-1}")
            if mi >= 4:
                wt(t, f"bwep{j}p{mi//2 - 2}")
            elif mi < 2:
                wt(t, f"bwep{j-1}p{2 + mi}")
            bank = PP[(mi // 2) % 2]
            half = mi % 2
            li = None
            for ki in range(NCD):
                li = t.matmul(bank[:, half * TB: half * TB + TB],
                              woT_s[:, ki * D + mi * 128: ki * D + mi * 128 + 128],
                              sh[par][:, ki * TB:(ki + 1) * TB],
                              start=(ki == 0), stop=(ki == NCD - 1),
                              skip_group_check=True)
            inc(li, "tsem", 1, f"bwo{j}m{mi}")

        def b_bwe_pair(j, p):
            @block.vector
            def _(v, j=j, p=p):
                v = _VSer(v)
                p3 = j % 3
                wt(v, f"bwo{j}m{2*p+1}")
                if p == 0:
                    wt(v, f"Bxr{j}")
                v.tensor_tensor(xB[p3][:, 2 * p * TB: (2 * p + 2) * TB],
                                xB[p3][:, 2 * p * TB: (2 * p + 2) * TB],
                                PP[p % 2][:, 0:2 * TB], op=ALU.add)
                vev(f"bwep{j}p{p}")

        def b_xb2sq2(j):
            @block.vector
            def _(v, j=j):
                v = _VSer(v)
                par = j % 2
                p3 = j % 3
                v.tensor_copy(rB[par][:, :], xB[p3][:, :])
                vev(f"bxb2{j}")

            @block.scalar
            def _(sc, j=j):
                par = j % 2
                p3 = j % 3
                wt(sc, f"bwep{j}p3")
                inc(sc.activation(sh[par][:, :], xB[p3][:, :], AF.Square),
                    "ssem", 1, f"bsq2{j}")

        def t_stats2(t, j):
            par = j % 2
            wt(t, f"bxb2{j}", f"bsq2{j}", f"brow{j}")
            for ki in range(NCD):
                t.matmul(PS1[0:1, 0:TB], ones_b[:, 0:1], rB[par][:, ki * TB:(ki + 1) * TB],
                         start=(ki == 0), stop=(ki == NCD - 1), skip_group_check=True)
            li = None
            for ki in range(NCD):
                li = t.matmul(PS1[0:1, TB:2 * TB], ones_b[:, 0:1],
                              sh[par][:, ki * TB:(ki + 1) * TB],
                              start=(ki == 0), stop=(ki == NCD - 1), skip_group_check=True)
            inc(li, "tsem", 1, f"bs2{j}")

        def b_rows2(j):
            @block.vector
            def _(v, j=j):
                v = _VSer(v)
                wt(v, f"bs2{j}", f"bsqr2{j-1}")
                v.tensor_scalar(rowB[0:1, RB_M2:RB_M2 + TB], PS1[0:1, 0:TB], INV_D, None,
                                op0=ALU.mult)
                v.tensor_scalar(rowB[0:1, RB_E2:RB_E2 + TB], PS1[0:1, TB:2 * TB], INV_D, EPS_LN,
                                op0=ALU.mult, op1=ALU.add)
                v.tensor_tensor(rowB[0:1, RB_T1:RB_T1 + TB], rowB[0:1, RB_M2:RB_M2 + TB],
                                rowB[0:1, RB_M2:RB_M2 + TB], op=ALU.mult)
                v.tensor_tensor(rowB[0:1, RB_T2:RB_T2 + TB], rowB[0:1, RB_E2:RB_E2 + TB],
                                rowB[0:1, RB_T1:RB_T1 + TB], op=ALU.subtract)
                v.reciprocal(rowB[0:1, RB_T1:RB_T1 + TB], rowB[0:1, RB_T2:RB_T2 + TB])
                vev(f"brow2{j}")

        def b_sqrt2(j):
            @block.scalar
            def _(sc, j=j):
                wt(sc, f"brow2{j}", f"bbc2{j-1}")
                inc(sc.activation(rowB[0:1, RB_RSTD2:RB_RSTD2 + TB],
                                  rowB[0:1, RB_T1:RB_T1 + TB], AF.Sqrt),
                    "ssem", 1, f"bsqr2{j}")

        def t_bcast2(t, j):
            wt(t, f"bsqr2{j}", f"bevacr{j}")
            t.matmul(PS2[:, 0:TB], ones_f[0:1, :], rowB[0:1, RB_M2:RB_M2 + TB],
                     start=True, stop=True, skip_group_check=True)
            inc(t.matmul(PS2[:, TB:2 * TB], ones_f[0:1, :],
                         rowB[0:1, RB_RSTD2:RB_RSTD2 + TB],
                         start=True, stop=True, skip_group_check=True),
                "tsem", 1, f"bbc2{j}")

        def b_evac2(j):
            @block.vector
            def _(v, j=j):
                v = _VSer(v)
                par = j % 2
                wt(v, f"bbc2{j}")
                v.tensor_copy(m2B_s[:, :], PS2[:, 0:TB])
                v.tensor_copy(rstd2B_s[:, :], PS2[:, TB:2 * TB])
                vev(f"bevac2{j}")

        def b_xn2(j):
            @block.vector
            def _(v, j=j):
                v = _VSer(v)
                par = j % 2
                p3 = j % 3
                wt(v, f"bevac2{j}", f"bs2{j}")
                for c in range(NCD):
                    sl = slice(c * TB, (c + 1) * TB)
                    v.tensor_tensor(wpB[par][:, sl], xB[p3][:, sl],
                                    m2B_s[:, :], op=ALU.subtract)
                for c in range(NCD):
                    sl = slice(c * TB, (c + 1) * TB)
                    v.tensor_tensor(sh[par][:, sl], wpB[par][:, sl],
                                    rstd2B_s[:, :], op=ALU.mult)
                vev(f"bxn2{j}")

        def b_w2g(gidx):
            @block.gpsimd
            def _(g, gidx=gidx):
                gp = gidx % 3
                wt(g, f"gfree{gidx-3}")
                d = g.dma_start(out=w2g[gp][:, :].rearrange("p (f e) -> p f e", f=4),
                                in_=scr_w2_G[gidx % 8])
                d.then_inc(SEM[f"g{gp}"], 16)
                CNT[f"g{gp}"] += 16
                EV[f"w2g{gidx}"] = (f"g{gp}", CNT[f"g{gp}"])

        FB = [PK1, PK2, PB1, PB2]

        def t_f1(t, j, hi):
            if hi == 0:
                wt(t, f"bxn2{j}", f"bfePKa{j-1}")
            if hi == 1:
                wt(t, f"bfePK{j-1}")
            if hi == 2:
                wt(t, f"bfePB{j-1}")
            if j == 0 and hi % 4 == 0:
                wt(t, f"w1c{hi // 4}")
            f1bank = FB[hi % 4]
            wt(t, f"g{j}h{hi-4}")
            par = j % 2
            li = None
            for ki in range(NCD):
                li = t.matmul(f1bank[:, 0:TB],
                              w1_s[:, hi * NCD * 128 + ki * 128: hi * NCD * 128 + ki * 128 + 128],
                              sh[par][:, ki * TB:(ki + 1) * TB],
                              start=(ki == 0), stop=(ki == NCD - 1),
                              skip_group_check=True)
            inc(li, "tsem", 1, f"f1{j}h{hi}")

        def s_gelu(j, hi):
            @block.scalar
            def _(sc, j=j, hi=hi):
                wt(sc, f"f1{j}h{hi}")
                f1bank = FB[hi % 4]
                inc(sc.activation(h_s[:, hi * TB:(hi + 1) * TB], f1bank[:, 0:TB], AF.Gelu,
                                  bias=b1_s[:, hi:hi + 1]), "ssem", 1, f"g{j}h{hi}")

        def t_f2open(t, j):
            wt(t, f"g{j}h{NHC-1}", "consts2", "carry0")
            for mi in range(NCD):
                bank = FB[mi // 2]
                half = mi % 2
                t.matmul(bank[:, half * TB: half * TB + TB],
                         b2b_s[0:1, mi * 128:(mi + 1) * 128], ones_r[0:1, :],
                         start=(half == 0), stop=False, skip_group_check=True)

        def t_f2(t, j, hi):
            gidx = j * 8 + hi // 4
            if hi % 4 == 0:
                wt(t, f"w2g{gidx}")
            gp = gidx % 3
            li = None
            for mi in range(NCD):
                bank = FB[mi // 2]
                half = mi % 2
                li = t.matmul(bank[:, half * TB: half * TB + TB],
                              w2g[gp][:, (hi % 4) * D + mi * 128: (hi % 4) * D + mi * 128 + 128],
                              h_s[:, hi * TB:(hi + 1) * TB],
                              start=False, stop=(hi == NHC - 1),
                              skip_group_check=True)
            inc(li, "tsem", 1, f"f2{j}h{hi}")
            if hi % 4 == 3:
                EV[f"gfree{gidx}"] = EV[f"f2{j}h{hi}"]

        def t_f2last(t, j):
            gidx = j * 8 + 7
            wt(t, f"w2g{gidx}")
            gp = gidx % 3
            li = None
            for mi in range(NCD):
                bank = FB[mi // 2]
                half = mi % 2
                for hi in range(NHC - 4, NHC):
                    li = t.matmul(bank[:, half * TB: half * TB + TB],
                                  w2g[gp][:, (hi % 4) * D + mi * 128: (hi % 4) * D + mi * 128 + 128],
                                  h_s[:, hi * TB:(hi + 1) * TB],
                                  start=False, stop=(hi == NHC - 1),
                                  skip_group_check=True)
                if mi == 1:
                    inc(li, "tsem", 1, f"f2PKa{j}")
                if mi == 3:
                    inc(li, "tsem", 1, f"f2PKb{j}")
            inc(li, "tsem", 1, f"f2{j}h{NHC-1}")
            EV[f"gfree{gidx}"] = EV[f"f2{j}h{NHC-1}"]

        def b_bfePK(j):
            @block.vector
            def _(v, j=j):
                v = _VSer(v)
                p3 = j % 3
                wt(v, f"f2PKa{j}")
                v.tensor_tensor(xB[p3][:, 0:2 * TB], xB[p3][:, 0:2 * TB],
                                PK1[:, 0:2 * TB], op=ALU.add)
                vev(f"bfePKa{j}")
                wt(v, f"f2PKb{j}")
                v.tensor_tensor(xB[p3][:, 2 * TB:4 * TB], xB[p3][:, 2 * TB:4 * TB],
                                PK2[:, 0:2 * TB], op=ALU.add)
                vev(f"bfePK{j}")

        def b_bfePB(j):
            @block.vector
            def _(v, j=j):
                v = _VSer(v)
                p3 = j % 3
                wt(v, f"f2{j}h{NHC-1}")
                v.tensor_tensor(xB[p3][:, 4 * TB:6 * TB], xB[p3][:, 4 * TB:6 * TB],
                                PB1[:, 0:2 * TB], op=ALU.add)
                v.tensor_tensor(xB[p3][:, 6 * TB:8 * TB], xB[p3][:, 6 * TB:8 * TB],
                                PB2[:, 0:2 * TB], op=ALU.add)
                vev(f"bfePB{j}")

        def b_ystore(j):
            @block.sync
            def _(sync, j=j):
                par = j % 2
                p3 = j % 3
                wt(sync, f"bfePB{j}", f"bfePK{j}")
                sy = f"y{par}"
                d = sync.dma_start(out=yout[j], in_=xB[p3][:, :])
                d.then_inc(SEM[sy], 16)
                CNT[sy] += 16
                EV[f"y{j}"] = (sy, CNT[sy])

        LATE_U = []
        b_loads_wu(0)
        b_loads_x(0)
        b_loads_wu(1)

        @block.sync
        def _(sync):
            wt(sync, "ccdone")
            d = sync.dma_start(out=G_s[:, :], in_=cc_out)
            d.then_inc(SEM["gss"], 16)
            CNT["gss"] += 16
            EV["Gs"] = ("gss", CNT["gss"])

        @block.tensor
        def _(t):
            wt(t, "Gs", f"xn{NTA-1}")
            li = None
            for c in range(NCD):
                li = t.matmul(PB1[:, c:c + 1], G_s[:, c * 128:(c + 1) * 128], sel_s[:, :],
                              start=True, stop=True, skip_group_check=True)
            inc(li, "tsem", 1, "sel")

        @block.vector
        def _(v):
            v = _VSer(v)
            wt(v, "sel")
            v.tensor_copy(carryB[:, :], PB1[:, 0:NCD])
            vev("cB")

        b_loads_x(1)
        b_scan(0)

        @block.scalar
        def _(sc):
            LATE_U[0](sc, 1)

        b_scan(1)
        b_sq(0)
        b_sq(1)

        @block.tensor
        def _(t):
            t_brs(t, 0)
        b_rows(0)
        b_sqrt(0)

        @block.tensor
        def _(t):
            t_bbc(t, 0)
        b_evacr(0)
        b_attn(0)
        for mi in range(NCD):
            @block.tensor
            def _(t, mi=mi):
                t_wo(t, 0, mi)
            if mi % 2 == 1:
                b_bwe_pair(0, mi // 2)
        b_xb2sq2(0)

        @block.tensor
        def _(t):
            t_stats2(t, 0)
        b_rows2(0)
        b_sqrt2(0)

        @block.tensor
        def _(t):
            t_bcast2(t, 0)
        b_evac2(0)
        b_xn2(0)

        @block.tensor
        def _(t):
            t_brs(t, 1)
        b_rows(1)
        b_sqrt(1)

        @block.tensor
        def _(t):
            t_bbc(t, 1)
        b_evacr(1)
        b_attn(1)

        for j in range(NTB):
            cj = j + 1
            c2 = j + 2

            if c2 < NTB:
                b_loads_wu(c2)
            b_loads_x(j + 2)

            for hi in range(NHC):
                @block.tensor
                def _(t, j=j, hi=hi):
                    t_f1(t, j, hi)
                s_gelu(j, hi)

                if cj < NTB:
                    if 2 <= hi <= 9:
                        @block.tensor
                        def _(t, cj=cj, mi=hi - 2):
                            t_wo(t, cj, mi)
                        if (hi - 2) % 2 == 1:
                            b_bwe_pair(cj, (hi - 2) // 2)
                    if hi == 10:
                        b_xb2sq2(cj)
                    if hi == 13:
                        @block.tensor
                        def _(t, cj=cj):
                            t_stats2(t, cj)
                        b_rows2(cj)
                    if hi == 16:
                        b_sqrt2(cj)
                    if hi == 18:
                        @block.tensor
                        def _(t, cj=cj):
                            t_bcast2(t, cj)
                        b_evac2(cj)
                        b_xn2(cj)

            @block.tensor
            def _(t, j=j):
                t_f2open(t, j)
            for g in range(8):
                @block.tensor
                def _(t, j=j, g=g):
                    if g == 7:
                        t_f2last(t, j)
                    else:
                        for hi in range(4 * g, 4 * g + 4):
                            t_f2(t, j, hi)
                if g + 3 < 8:
                    b_w2g(j * 8 + g + 3)
                if c2 < NTB:
                    if g == 0:
                        b_scan(c2)
                    if g == 1:
                        b_sq(c2)
                    if g == 4:
                        @block.tensor
                        def _(t, c2=c2):
                            t_brs(t, c2)
                        b_rows(c2)
                        b_sqrt(c2)
                    if g == 6:
                        @block.tensor
                        def _(t, c2=c2):
                            t_bbc(t, c2)
                        b_evacr(c2)
                        b_attn(c2)

            if j + 1 < NTB:
                for gg in range(3):
                    b_w2g((j + 1) * 8 + gg)

            b_bfePK(j)
            b_bfePB(j)
            b_ystore(j)

        @block.sync
        def _(sync):
            wt(sync, f"y{NTB-1}", f"y{NTB-2}")

        fake = _FakeEng()
        for _name, _fn in PROG:
            _fn(fake)

        PASS[0] = 2
        for _k in CNT:
            CNT[_k] = 0
        with nc.Block() as realblock:
            for _eng_name in ENGS:
                def _run(eng, _eng_name=_eng_name):
                    for _n2, _fn in PROG:
                        if _n2 == _eng_name:
                            _fn(eng)
                getattr(realblock, _eng_name)(_run)

    return nc


def kernel(L0_attn_x, L0_attn_Wr, L0_attn_Wk, L0_attn_Wv, L0_attn_Ww,
           L0_attn_Wo, L0_ffn_w1, L0_ffn_b1, L0_ffn_w2, L0_ffn_b2):
    x = np.ascontiguousarray(np.asarray(L0_attn_x, dtype=np.float32))
    Wr = np.asarray(L0_attn_Wr, dtype=np.float32)
    Wk = np.asarray(L0_attn_Wk, dtype=np.float32)
    Wv = np.asarray(L0_attn_Wv, dtype=np.float32)
    Ww = np.asarray(L0_attn_Ww, dtype=np.float32)
    Wo = np.asarray(L0_attn_Wo, dtype=np.float32)
    w1 = np.ascontiguousarray(np.asarray(L0_ffn_w1, dtype=np.float32))
    b1 = np.asarray(L0_ffn_b1, dtype=np.float32)
    w2 = np.ascontiguousarray(np.asarray(L0_ffn_w2, dtype=np.float32))
    b2 = np.asarray(L0_ffn_b2, dtype=np.float32)

    wrT = np.ascontiguousarray(Wr.T)
    wvT = np.ascontiguousarray(Wv.T)
    wwT = np.ascontiguousarray(Ww.T)
    woT = np.ascontiguousarray(Wo.T)
    wkN = np.ascontiguousarray(Wk)
    w1n = np.ascontiguousarray(
        w1.reshape(NCD, 128, NHC, 128).transpose(1, 2, 0, 3).reshape(128, NHC * NCD * 128))
    b1r = np.ascontiguousarray(b1.reshape(NHC, 128).T)
    b2row = np.ascontiguousarray(b2.reshape(1, D))
    dd = np.arange(D)
    emT = np.ascontiguousarray((dd[:, None] // HD == np.arange(NH)[None, :]).astype(np.float32))
    emask = np.ascontiguousarray(emT.T)
    onesf = np.ones((128, 128), np.float32)

    in_maps = []
    for c in range(NCORES):
        b, h = c // 2, c % 2
        xs = x[b, h * TC:(h + 1) * TC, :]
        xin = np.ascontiguousarray(
            xs.reshape(NTA, TA, NCD, 128).transpose(0, 3, 2, 1).reshape(NTA, 128, NCD * TA))
        sel = np.zeros((2, 1), np.float32)
        if h == 1:
            sel[0, 0] = 1.0
        in_maps.append({
            "xin": xin, "wrT": wrT, "wvT": wvT, "wwT": wwT, "woT": woT,
            "wkN": wkN, "w1n": w1n, "w2": w2, "b1r": b1r, "b2row": b2row,
            "emT": emT, "emask": emask, "onesf": onesf, "selv": sel,
        })

    nc = build_nc()
    res = run_bass_kernel_spmd(nc, in_maps, core_ids=list(range(NCORES)))

    out = np.empty((B, T, D), np.float32)
    for c in range(NCORES):
        b, h = c // 2, c % 2
        yc = np.asarray(res.results[c]["yout"])
        blk = yc.reshape(NTB, 128, NCD, TB).transpose(0, 3, 2, 1).reshape(TC, D)
        out[b, h * TC:(h + 1) * TC, :] = blk
    return out


# version 18
# speedup vs baseline: 1.0359x; 1.0359x over previous
import contextlib
import re

import numpy as np

import concourse.bass as bass
import concourse.mybir as mybir
from concourse.bass_utils import run_bass_kernel_spmd

F32 = mybir.dt.float32
BF16 = mybir.dt.bfloat16
AF = mybir.ActivationFunctionType
ALU = mybir.AluOpType

B, T, D = 4, 8192, 1024
NH, HD = 16, 64
NCORES = 8
TC = T // 2
NCD = 8
TA = 512
NTA = TC // TA
TB = 256
NTB = TC // TB
DFF = 4096
NHC = DFF // 128
EPS_LN = 1e-5
EPS_RMS = float(np.finfo(np.float32).eps)
INV_D = 1.0 / D

RA_M, RA_E2, RA_T1, RA_T2 = 0, 512, 1024, 1536
RB_MSR, RB_REC, RB_RSTD = 0, 256, 512
RB_M2, RB_E2, RB_T1, RB_T2, RB_RSTD2 = 768, 1024, 1280, 1536, 1792


ENGS = ("sync", "gpsimd", "scalar", "tensor", "vector")


class _FakeInstr:
    __slots__ = ()

    def then_inc(self, sem, n):
        return self


class _FakeEng:
    __slots__ = ()
    _FI = _FakeInstr()

    def __getattr__(self, name):
        def f(*a, **k):
            return _FakeEng._FI
        return f


def build_nc():
    nc = bass.Bass("TRN2", target_bir_lowering=False, detect_race_conditions=False)

    xin = nc.dram_tensor("xin", [NTA, 128, NCD * TA], F32, kind="ExternalInput").ap()
    wrT = nc.dram_tensor("wrT", [D, D], F32, kind="ExternalInput").ap()
    wvT = nc.dram_tensor("wvT", [D, D], F32, kind="ExternalInput").ap()
    wwT = nc.dram_tensor("wwT", [D, D], F32, kind="ExternalInput").ap()
    woT = nc.dram_tensor("woT", [D, D], F32, kind="ExternalInput").ap()
    wkN = nc.dram_tensor("wkN", [D, D], F32, kind="ExternalInput").ap()
    w1n = nc.dram_tensor("w1n", [128, NHC * NCD * 128], F32, kind="ExternalInput").ap()
    w2 = nc.dram_tensor("w2", [DFF, D], F32, kind="ExternalInput").ap()
    b1r = nc.dram_tensor("b1r", [128, NHC], F32, kind="ExternalInput").ap()
    b2row = nc.dram_tensor("b2row", [1, D], F32, kind="ExternalInput").ap()
    emT = nc.dram_tensor("emT", [D, NH], F32, kind="ExternalInput").ap()
    emask = nc.dram_tensor("emask", [NH, D], F32, kind="ExternalInput").ap()
    onesf = nc.dram_tensor("onesf", [128, 128], F32, kind="ExternalInput").ap()
    selv = nc.dram_tensor("selv", [2, 1], F32, kind="ExternalInput").ap()

    yout = nc.dram_tensor("yout", [NTB, 128, NCD * TB], F32, kind="ExternalOutput").ap()

    scr_w = nc.dram_tensor("scr_w", [NTA, 128, NCD * TA], F32).ap()
    scr_u = nc.dram_tensor("scr_u", [NTA, 128, NCD * TA], F32).ap()
    scr_r = nc.dram_tensor("scr_r", [NTA, 128, NCD * TA], BF16).ap()
    scr_w2 = nc.dram_tensor("scr_w2", [NHC, 128, D], BF16).ap()
    cc_in = nc.dram_tensor("cc_in", [1, D], F32).ap()
    cc_out = nc.dram_tensor("cc_out", [2, D], F32).ap()

    xin_B = xin.rearrange("n p (c two t) -> n two p c t", two=2, t=TB)
    scr_w_B = scr_w.rearrange("n p (c two t) -> n two p c t", two=2, t=TB)
    scr_u_B = scr_u.rearrange("n p (c two t) -> n two p c t", two=2, t=TB)
    scr_r_B = scr_r.rearrange("n p (c two t) -> n two p c t", two=2, t=TB)
    scr_w2_G = scr_w2.rearrange("(g f) p e -> g p f e", f=4)
    wrT_L = wrT.rearrange("(k p) i -> p k i", p=128)
    wvT_L = wvT.rearrange("(k p) i -> p k i", p=128)
    wwT_L = wwT.rearrange("(k p) i -> p k i", p=128)
    woT_L = woT.rearrange("(k p) i -> p k i", p=128)
    wkN_L = wkN.rearrange("(k p) e -> p k e", p=128)
    emT_L = emT.rearrange("(k p) h -> p k h", p=128)

    es = contextlib.ExitStack()
    with es:
        ones_f = es.enter_context(nc.sbuf_tensor("ones_f", [128, 128], F32))
        ones_b = es.enter_context(nc.sbuf_tensor("ones_b", [128, 128], BF16))
        sel_s = es.enter_context(nc.sbuf_tensor("sel_s", [2, 1], F32))
        b1_s = es.enter_context(nc.sbuf_tensor("b1_s", [128, NHC], F32))
        b2b_s = es.enter_context(nc.sbuf_tensor("b2b_s", [1, D], BF16))
        ones_r = es.enter_context(nc.sbuf_tensor("ones_r", [1, TB], BF16))
        G_s = es.enter_context(nc.sbuf_tensor("G_s", [2, D], F32))
        carryA = es.enter_context(nc.sbuf_tensor("carryA", [128, NCD], F32))
        carryB = es.enter_context(nc.sbuf_tensor("carryB", [128, NCD], F32))
        rowA = es.enter_context(nc.sbuf_tensor("rowA", [1, 2048], F32))
        rowB = rowA


        PS1 = es.enter_context(nc.psum_tensor("PS1", [128, 512], F32))
        PS2 = es.enter_context(nc.psum_tensor("PS2", [128, 512], F32))
        PB1 = es.enter_context(nc.psum_tensor("PB1", [128, 512], F32))
        PB2 = es.enter_context(nc.psum_tensor("PB2", [128, 512], F32))
        PP = [es.enter_context(nc.psum_tensor(f"PP{k}", [128, 512], F32)) for k in range(2)]
        PK1 = es.enter_context(nc.psum_tensor("PK1", [128, 512], F32))
        PK2 = es.enter_context(nc.psum_tensor("PK2", [128, 512], F32))

        SEMN = ["tsem", "vsem", "ssem", "csem", "wt", "wtg", "wtw", "xb", "cc", "y",
                "wk0", "wk1", "w1s", "wos", "gss", "y0", "y1",
                "x0", "x1", "st0", "st1",
                "w2l0", "w2l1", "w2l2", "w2l3", "w2s0", "w2s1", "w2s2", "w2s3",
                "bx0", "bx1", "bx2", "bs0", "bs1", "g0", "g1", "g2"]
        SEM = {n: es.enter_context(nc.semaphore(n)) for n in SEMN}
        CNT = {n: 0 for n in SEMN}
        EV = {}
        PASS = [1]

        def inc(instr, s, n, evname=None):
            CNT[s] += n
            instr.then_inc(SEM[s], n)
            if evname is not None:
                if evname in EV and PASS[0] == 2:
                    assert EV[evname] == (s, CNT[s]), \
                        f"event {evname} diverged between passes"
                EV[evname] = (s, CNT[s])
            return instr

        def wt(eng, *names):
            for name in names:
                if name in EV:
                    s, v = EV[name]
                    eng.wait_ge(SEM[s], v)
                else:
                    assert name == "none" or re.search(r"-\d", name), \
                        f"wait on event {name!r} that was never recorded"

        _VOPS = ("tensor_tensor", "tensor_scalar", "tensor_copy", "reciprocal",
                 "tensor_tensor_scan", "memset", "scalar_tensor_tensor")

        class _VSer:
            __slots__ = ("_v",)

            def __init__(self, v):
                self._v = v

            def __getattr__(self, name):
                f = getattr(self._v, name)
                if name in _VOPS:
                    def wrapped(*a, **k):
                        self._v.wait_ge(SEM["vsem"], CNT["vsem"])
                        r = f(*a, **k)
                        CNT["vsem"] += 1
                        r.then_inc(SEM["vsem"], 1)
                        return r
                    return wrapped
                return f

        def vev(*names):
            for name in names:
                if name in EV and PASS[0] == 2:
                    assert EV[name] == ("vsem", CNT["vsem"]), \
                        f"event {name} diverged between passes"
                EV[name] = ("vsem", CNT["vsem"])

        PROG = []

        class _Sec:
            def __getattr__(self, eng_name):
                def reg(fn):
                    PROG.append((eng_name, fn))
                return reg

        block = _Sec()

        @block.sync
        def _(sync):
            sync.dma_start(out=ones_f[:, :], in_=onesf).then_inc(SEM["wt"], 16)
            sync.dma_start(out=sel_s[:, :], in_=selv).then_inc(SEM["wt"], 16)
            sync.dma_start(out=b1_s[:, :], in_=b1r).then_inc(SEM["wt"], 16)
            CNT["wt"] += 48
            EV["consts"] = ("wt", CNT["wt"])

        @block.gpsimd
        def _(g):
            g.dma_start(out=ones_b[:, :], in_=onesf).then_inc(SEM["wtg"], 16)
            g.dma_start(out=emask_s[:, :], in_=emask).then_inc(SEM["wtg"], 16)
            g.dma_start(out=emT_s[:, :].rearrange("p (k h) -> p k h", k=NCD),
                        in_=emT_L).then_inc(SEM["wtg"], 16)
            g.dma_start(out=b2b_s[:, :], in_=b2row).then_inc(SEM["wtg"], 16)
            CNT["wtg"] += 64
            EV["consts2"] = ("wtg", CNT["wtg"])

        @block.vector
        def _(v):
            v = _VSer(v)
            v.memset(carryA[:, :], 0.0)
            v.memset(ones_r[:, :], 1.0)
            vev("carry0")

        esA = contextlib.ExitStack()
        K_s = esA.enter_context(nc.sbuf_tensor("K_s", [16, TA], BF16))
        emask_s = esA.enter_context(nc.sbuf_tensor("emask_s", [16, D], BF16))
        emT_s = esA.enter_context(nc.sbuf_tensor("emT_s", [128, NCD * 16], BF16))
        wkhT_s = esA.enter_context(nc.sbuf_tensor("wkhT_s", [128, NCD * 16], BF16))
        WrT_s = esA.enter_context(nc.sbuf_tensor("WrT_s", [128, NCD * D], BF16))
        WvT_s = esA.enter_context(nc.sbuf_tensor("WvT_s", [128, NCD * D], BF16))
        WwT_s = esA.enter_context(nc.sbuf_tensor("WwT_s", [128, NCD * D], BF16))
        wkst = [esA.enter_context(nc.sbuf_tensor(f"wkst{k}", [128, D], BF16)) for k in range(2)]
        w2st = [esA.enter_context(nc.sbuf_tensor(f"w2st{k}", [128, D], BF16)) for k in range(4)]
        xA = [esA.enter_context(nc.sbuf_tensor(f"xA{k}", [128, NCD * TA], F32)) for k in range(2)]
        xbA = esA.enter_context(nc.sbuf_tensor("xbA", [128, NCD * TA], BF16))
        xsqA = esA.enter_context(nc.sbuf_tensor("xsqA", [128, NCD * TA], BF16))
        xnA = [esA.enter_context(nc.sbuf_tensor(f"xnA{k}", [128, NCD * TA], BF16)) for k in range(2)]
        sigw = esA.enter_context(nc.sbuf_tensor("sigw", [128, NCD * TA], F32))
        ubuf = esA.enter_context(nc.sbuf_tensor("ubuf", [128, NCD * TA], F32))
        vbf = esA.enter_context(nc.sbuf_tensor("vbf", [128, NCD * TA], BF16))
        rbf = esA.enter_context(nc.sbuf_tensor("rbf", [128, NCD * TA], BF16))
        sacr = esA.enter_context(nc.sbuf_tensor("sacr", [128, TA], F32))

        @block.gpsimd
        def _(g):
            g.dma_start(out=WrT_s[:, :].rearrange("p (k i) -> p k i", k=NCD),
                        in_=wrT_L).then_inc(SEM["wtw"], 16)
            g.dma_start(out=WvT_s[:, :].rearrange("p (k i) -> p k i", k=NCD),
                        in_=wvT_L).then_inc(SEM["wtw"], 16)
            g.dma_start(out=WwT_s[:, :].rearrange("p (k i) -> p k i", k=NCD),
                        in_=wwT_L).then_inc(SEM["wtw"], 16)
            CNT["wtw"] += 48
            EV["wts"] = ("wtw", CNT["wtw"])

        for ki in range(NCD):
            @block.gpsimd
            def _(g, ki=ki):
                wt(g, f"wkmm{ki-2}")
                sw = f"wk{ki % 2}"
                d = g.dma_start(out=wkst[ki % 2][:, :], in_=wkN_L[:, ki, :])
                d.then_inc(SEM[sw], 16)
                CNT[sw] += 16
                EV[f"wkld{ki}"] = (sw, CNT[sw])

            @block.tensor
            def _(t, ki=ki):
                if ki == 0:
                    wt(t, "consts2", "wts")
                wt(t, f"wkld{ki}")
                li = None
                for mi in range(NCD):
                    li = t.matmul(PK1[:, mi * 16:(mi + 1) * 16],
                                  wkst[ki % 2][:, mi * 128: mi * 128 + 128],
                                  emT_s[:, ki * 16:(ki + 1) * 16],
                                  start=(ki == 0 and mi == 0), stop=(ki == NCD - 1),
                                  skip_group_check=True)
                inc(li, "tsem", 1, f"wkmm{ki}")

        @block.vector
        def _(v):
            v = _VSer(v)
            wt(v, f"wkmm{NCD-1}")
            v.tensor_copy(wkhT_s[:, :], PK1[:, 0:128])
            vev("wkh")


        RQ = {"q": 0}
        qevent = {}

        def a_xload(i):
            if i >= NTA:
                return

            @block.sync
            def _(sync, i=i):
                wt(sync, f"xn{i-2}", f"xsq{i-2}")
                d = sync.dma_start(out=xA[i % 2][:, :], in_=xin[i])
                d.then_inc(SEM[f"x{i % 2}"], 16)
                CNT[f"x{i % 2}"] += 16
                EV[f"xA{i}"] = (f"x{i % 2}", CNT[f"x{i % 2}"])

        def a_xbload(i):
            if i >= NTA:
                return

            @block.gpsimd
            def _(g, i=i):
                wt(g, f"stats{i-1}")
                d = g.dma_start(out=xbA[:, :], in_=xin[i])
                d.then_inc(SEM["xb"], 16)
                CNT["xb"] += 16
                EV[f"xb{i}"] = ("xb", CNT["xb"])

        def a_xsq_half(i, h):
            if i >= NTA:
                return

            @block.vector
            def _(v, i=i, h=h):
                v = _VSer(v)
                if h == 0:
                    wt(v, f"xA{i}", f"stats{i-1}")
                sl = slice(h * 4 * TA, (h + 1) * 4 * TA)
                v.tensor_tensor(xsqA[:, sl], xA[i % 2][:, sl], xA[i % 2][:, sl],
                                op=ALU.mult)
                if h == 1:
                    vev(f"xsq{i}")

        def a_stats(i):
            if i >= NTA:
                return

            @block.tensor
            def _(t, i=i):
                wt(t, f"xb{i}", f"xsq{i}", f"rows{i-1}")
                for ki in range(NCD):
                    t.matmul(PS1[0:1, 0:TA], ones_b[:, 0:1], xbA[:, ki * TA:(ki + 1) * TA],
                             start=(ki == 0), stop=(ki == NCD - 1))
                li = None
                for ki in range(NCD):
                    li = t.matmul(PS2[0:1, 0:TA], ones_b[:, 0:1], xsqA[:, ki * TA:(ki + 1) * TA],
                                  start=(ki == 0), stop=(ki == NCD - 1))
                inc(li, "tsem", 1, f"stats{i}")

        def a_rows(i):
            if i >= NTA:
                return

            @block.vector
            def _(v, i=i):
                v = _VSer(v)
                wt(v, f"stats{i}", f"sqrt{i-1}", f"bcast{i-1}")
                v.tensor_scalar(rowA[0:1, RA_M:RA_M + TA], PS1[0:1, 0:TA], INV_D, None,
                                op0=ALU.mult)
                v.tensor_scalar(rowA[0:1, RA_E2:RA_E2 + TA], PS2[0:1, 0:TA], INV_D, EPS_LN,
                                op0=ALU.mult, op1=ALU.add)
                v.tensor_tensor(rowA[0:1, RA_T1:RA_T1 + TA], rowA[0:1, RA_M:RA_M + TA],
                                rowA[0:1, RA_M:RA_M + TA], op=ALU.mult)
                v.tensor_tensor(rowA[0:1, RA_T2:RA_T2 + TA], rowA[0:1, RA_E2:RA_E2 + TA],
                                rowA[0:1, RA_T1:RA_T1 + TA], op=ALU.subtract)
                v.reciprocal(rowA[0:1, RA_T1:RA_T1 + TA], rowA[0:1, RA_T2:RA_T2 + TA])
                vev(f"rows{i}")

        def a_sqrt(i):
            if i >= NTA:
                return

            @block.scalar
            def _(sc, i=i):
                wt(sc, f"rows{i}")
                inc(sc.activation(rowA[0:1, RA_T2:RA_T2 + TA], rowA[0:1, RA_T1:RA_T1 + TA],
                                  AF.Sqrt), "ssem", 1, f"sqrt{i}")

        def a_bcast(i):
            if i >= NTA:
                return

            @block.tensor
            def _(t, i=i):
                wt(t, "consts", f"sqrt{i}", f"xn{i-1}")
                t.matmul(PB1[:, 0:TA], ones_f[0:1, :], rowA[0:1, RA_M:RA_M + TA],
                         start=True, stop=True)
                li = t.matmul(PB2[:, 0:TA], ones_f[0:1, :], rowA[0:1, RA_T2:RA_T2 + TA],
                              start=True, stop=True)
                inc(li, "tsem", 1, f"bcast{i}")

        def a_xn_half(i, h):
            if i >= NTA:
                return

            @block.vector
            def _(v, i=i, h=h):
                v = _VSer(v)
                if h == 0:
                    wt(v, f"bcast{i}", f"xA{i}", f"pr{i-2}m{NCD-1}")
                for c in range(4 * h, 4 * h + 4):
                    sl = slice(c * TA, (c + 1) * TA)
                    v.tensor_tensor(xA[i % 2][:, sl], xA[i % 2][:, sl], PB1[:, 0:TA],
                                    op=ALU.subtract)
                    v.tensor_tensor(xnA[i % 2][:, sl], xA[i % 2][:, sl], PB2[:, 0:TA],
                                    op=ALU.mult)
                if h == 1:
                    vev(f"xn{i}")

        def a_xn(i):
            a_xn_half(i, 0)
            a_xn_half(i, 1)

        a_xload(0)
        a_xload(1)
        a_xbload(0)
        a_xsq_half(0, 0)
        a_xsq_half(0, 1)
        a_stats(0)
        a_xbload(1)
        a_rows(0)
        a_sqrt(0)
        a_bcast(0)
        a_xn(0)

        def t_kb(t, i, cc):
            wt(t, f"Kev{i}", f"u{i}c{cc-1}" if cc > 0 else f"u{i-1}c{NCD-1}")
            inc(t.matmul(PK2[:, 0:TA], emask_s[:, cc * 128:(cc + 1) * 128],
                         K_s[:, :], start=True, stop=True),
                "tsem", 1, f"kb{i}c{cc}")

        def b_uscan(i, c):
            @block.vector
            def _(v, i=i, c=c):
                v = _VSer(v)
                sl = slice(c * TA, (c + 1) * TA)
                wt(v, f"vb{i}c{c}", f"kb{i}c{c}", f"stu{i-1}")
                v.tensor_tensor(ubuf[:, sl], vbf[:, sl], PK2[:, 0:TA], op=ALU.mult)
                vev(f"u{i}c{c}")
                if c == 0:
                    wt(v, "carry0")
                v.tensor_tensor_scan(sacr[:, :], sigw[:, sl], ubuf[:, sl],
                                     carryA[:, c:c + 1], op0=ALU.mult, op1=ALU.add)
                v.tensor_copy(carryA[:, c:c + 1], sacr[:, TA - 1: TA])
                if c == NCD - 1:
                    vev(f"scan{i}")

        ROTW = [PP[0], PP[1], PK1]

        def emit_r(i, deferred=False):
            sst = f"st{i % 2}"
            for mi in range(NCD):
                bank = ROTW[RQ["q"] % 3]
                wait_prev = qevent.get(RQ["q"] - 3, "none")
                qevent[RQ["q"]] = f"r{i}m{mi}"
                RQ["q"] += 1

                @block.tensor
                def _(t, i=i, mi=mi, bank=bank, wait_prev=wait_prev):
                    wt(t, wait_prev)
                    li = None
                    for ki in range(NCD):
                        li = t.matmul(bank[:, 0:TA],
                                      WrT_s[:, ki * D + mi * 128: ki * D + mi * 128 + 128],
                                      xnA[i % 2][:, ki * TA:(ki + 1) * TA],
                                      start=(ki == 0), stop=(ki == NCD - 1))
                    inc(li, "tsem", 1, f"pr{i}m{mi}")

                @block.scalar
                def _(sc, i=i, mi=mi, bank=bank):
                    wt(sc, f"pr{i}m{mi}", f"str{i-1}")
                    inc(sc.activation(rbf[:, mi * TA:(mi + 1) * TA],
                                      bank[:, 0:TA], AF.Sigmoid),
                        "ssem", 1, f"r{i}m{mi}")

                if deferred:
                    @block.scalar
                    def _(sc, i=i, mi=mi, sst=sst):
                        wt(sc, f"r{i}m{mi}")
                        d = sc.dma_start(out=scr_r[i][:, mi * TA:(mi + 1) * TA],
                                         in_=rbf[:, mi * TA:(mi + 1) * TA])
                        d.then_inc(SEM[sst], 16)
                        CNT[sst] += 16
                        if mi == NCD - 1:
                            EV[f"str{i}"] = (sst, CNT[sst])

        for i in range(NTA):
            par = i % 2
            sst = f"st{par}"

            a_xload(i + 2)

            @block.gpsimd
            def _(g, i=i):
                for k in range(4):
                    s = i * 4 + k
                    sp = s % 4
                    wt(g, f"w2s{s-4}")
                    d = g.dma_start(out=w2st[s % 4][:, :], in_=w2[s * 128:(s + 1) * 128, :])
                    d.then_inc(SEM[f"w2l{sp}"], 16)
                    CNT[f"w2l{sp}"] += 16
                    EV[f"w2l{s}"] = (f"w2l{sp}", CNT[f"w2l{sp}"])

            @block.sync
            def _(sync, i=i):
                for k in range(4):
                    s = i * 4 + k
                    sp = s % 4
                    wt(sync, f"w2l{s}")
                    d = sync.dma_start(out=scr_w2[s], in_=w2st[s % 4][:, :])
                    d.then_inc(SEM[f"w2s{sp}"], 16)
                    CNT[f"w2s{sp}"] += 16
                    EV[f"w2s{s}"] = (f"w2s{sp}", CNT[f"w2s{sp}"])

            @block.tensor
            def _(t, i=i):
                wt(t, "wkh", f"Kev{i-1}", f"u{i-1}c{NCD-1}", f"xn{i}")
                li = None
                for ki in range(NCD):
                    li = t.matmul(PK2[0:16, 0:TA], wkhT_s[:, ki * 16:(ki + 1) * 16],
                                  xnA[i % 2][:, ki * TA:(ki + 1) * TA],
                                  start=(ki == 0), stop=(ki == NCD - 1))
                inc(li, "tsem", 1, f"K{i}")

            @block.scalar
            def _(sc, i=i):
                wt(sc, f"K{i}")
                inc(sc.copy(K_s[:, :], PK2[0:16, 0:TA]), "ssem", 1, f"Kev{i}")

            for c in range(NCD):
                bank = ROTW[RQ["q"] % 3]
                wait_prev = qevent.get(RQ["q"] - 3, "none")
                qevent[RQ["q"]] = f"sgw{i}m{c}"
                RQ["q"] += 1

                @block.tensor
                def _(t, i=i, c=c, bank=bank, wait_prev=wait_prev):
                    if c == 0:
                        wt(t, f"xn{i}")
                    wt(t, wait_prev)
                    if bank is PK1:
                        wt(t, "none")
                    li = None
                    for ki in range(NCD):
                        li = t.matmul(bank[:, 0:TA],
                                      WwT_s[:, ki * D + c * 128: ki * D + c * 128 + 128],
                                      xnA[i % 2][:, ki * TA:(ki + 1) * TA],
                                      start=(ki == 0), stop=(ki == NCD - 1))
                    inc(li, "tsem", 1, f"pw{i}m{c}")

                @block.scalar
                def _(sc, i=i, c=c, bank=bank):
                    wt(sc, f"pw{i}m{c}", f"stw{i-1}", f"scan{i-1}")
                    inc(sc.activation(sigw[:, c * TA:(c + 1) * TA],
                                      bank[:, 0:TA], AF.Sigmoid),
                        "ssem", 1, f"sgw{i}m{c}")

                @block.vector
                def _(v, i=i, c=c):
                    v = _VSer(v)
                    wt(v, f"sgw{i}m{c}")
                    v.tensor_scalar(sigw[:, c * TA:(c + 1) * TA],
                                    sigw[:, c * TA:(c + 1) * TA], 0.99, None,
                                    op0=ALU.mult)
                    vev(f"wpc{i}m{c}")

                bank2 = ROTW[RQ["q"] % 3]
                wait_prev2 = qevent.get(RQ["q"] - 3, "none")
                qevent[RQ["q"]] = f"vb{i}c{c}"
                RQ["q"] += 1

                @block.tensor
                def _(t, i=i, c=c, bank2=bank2, wait_prev2=wait_prev2):
                    wt(t, wait_prev2)
                    li = None
                    for ki in range(NCD):
                        li = t.matmul(bank2[:, 0:TA],
                                      WvT_s[:, ki * D + c * 128: ki * D + c * 128 + 128],
                                      xnA[i % 2][:, ki * TA:(ki + 1) * TA],
                                      start=(ki == 0), stop=(ki == NCD - 1))
                    inc(li, "tsem", 1, f"pv{i}c{c}")
                    t_kb(t, i, c)

                @block.scalar
                def _(sc, i=i, c=c, bank2=bank2):
                    wt(sc, f"pv{i}c{c}", f"u{i-1}c{c}")
                    inc(sc.copy(vbf[:, c * TA:(c + 1) * TA], bank2[:, 0:TA]),
                        "ssem", 1, f"vb{i}c{c}")

                if c == 7:
                    a_rows(i + 1)

                b_uscan(i, c)

                if c == 3:
                    a_xsq_half(i + 1, 0)
                if c == 5:
                    a_xsq_half(i + 1, 1)
                if c == 6:
                    a_stats(i + 1)

            a_xbload(i + 2)
            a_sqrt(i + 1)
            a_bcast(i + 1)
            a_xn_half(i + 1, 0)
            a_xn_half(i + 1, 1)

            if i < NTA - 2:
                emit_r(i)

            @block.sync
            def _(sync, i=i, sst=sst):
                wt(sync, f"wpc{i}m{NCD-1}")
                sync.dma_start(out=scr_w[i], in_=sigw[:, :]).then_inc(SEM[sst], 16)
                CNT[sst] += 16
                wt(sync, f"u{i}c{NCD-1}")
                sync.dma_start(out=scr_u[i], in_=ubuf[:, :]).then_inc(SEM[sst], 16)
                CNT[sst] += 16
                EV[f"stw{i}"] = (sst, CNT[sst])
                EV[f"stu{i}"] = (sst, CNT[sst])
                if i < NTA - 2:
                    wt(sync, f"r{i}m{NCD-1}")
                    sync.dma_start(out=scr_r[i], in_=rbf[:, :]).then_inc(SEM[sst], 16)
                    CNT[sst] += 16
                    EV[f"str{i}"] = (sst, CNT[sst])

        @block.sync
        def _(sync):
            wt(sync, f"scan{NTA-1}")
            for c in range(NCD):
                d = sync.dma_start(out=cc_in[0, c * 128:(c + 1) * 128], in_=carryA[:, c:c + 1])
                d.then_inc(SEM["cc"], 16)
            CNT["cc"] += 16 * NCD
            EV["carry"] = ("cc", CNT["cc"])

        @block.gpsimd
        def _(g):
            wt(g, "carry")
            cci = g.collective_compute(
                "AllGather", ALU.bypass,
                replica_groups=[[2 * p, 2 * p + 1] for p in range(NCORES // 2)],
                ins=[cc_in], outs=[cc_out],
            )
            cci.then_inc(SEM["csem"], 1)
            CNT["csem"] += 1
            EV["ccdone"] = ("csem", CNT["csem"])

        emit_r(NTA - 2, deferred=True)
        emit_r(NTA - 1, deferred=True)

        A_EARLY = [f"pr{NTA-1}m{NCD-1}", f"scan{NTA-1}",
                   f"sgw{NTA-1}m{NCD-1}", f"xn{NTA-1}", f"xn{NTA-2}",
                   f"u{NTA-1}c{NCD-1}", f"stats{NTA-1}", f"xsq{NTA-1}",
                   f"stw{NTA-1}", f"stw{NTA-2}",
                   f"w2s{4*NTA-1}", f"w2s{4*NTA-2}", f"w2s{4*NTA-3}", f"w2s{4*NTA-4}",
                   f"xA{NTA-1}", f"xA{NTA-2}", f"xb{NTA-1}",
                   f"wkld{NCD-1}", f"wkld{NCD-2}"]
        A_DONE = A_EARLY + [f"r{NTA-1}m{NCD-1}", f"str{NTA-1}", f"str{NTA-2}"]

        esA.close()

        w1_s = es.enter_context(nc.sbuf_tensor("w1_s", [128, NCD * DFF], BF16))
        woT_s = es.enter_context(nc.sbuf_tensor("woT_s", [128, NCD * D], BF16))
        w2g = [es.enter_context(nc.sbuf_tensor(f"w2g{k}", [128, 4 * D], BF16)) for k in range(3)]
        xB = [es.enter_context(nc.sbuf_tensor(f"xB{k}", [128, NCD * TB], F32)) for k in range(3)]
        wpB = [es.enter_context(nc.sbuf_tensor(f"wpB{k}", [128, NCD * TB], F32)) for k in range(2)]
        uB = es.enter_context(nc.sbuf_tensor("uB", [128, NCD * TB], F32))
        rB = [es.enter_context(nc.sbuf_tensor(f"rB{k}", [128, NCD * TB], BF16)) for k in range(2)]
        sB = es.enter_context(nc.sbuf_tensor("sB", [128, TB], F32))
        sh = [es.enter_context(nc.sbuf_tensor(f"sh{k}", [128, NCD * TB], BF16)) for k in range(2)]
        h_s = es.enter_context(nc.sbuf_tensor("h_s", [128, NHC * TB], BF16))
        rstdB_s = es.enter_context(nc.sbuf_tensor("rstdB_s", [128, TB], F32))
        m2B_s = es.enter_context(nc.sbuf_tensor("m2B_s", [128, TB], F32))
        rstd2B_s = es.enter_context(nc.sbuf_tensor("rstd2B_s", [128, TB], F32))

        @block.gpsimd
        def _(g):
            wt(g, *A_EARLY)
            g.dma_start(out=woT_s[:, :].rearrange("p (k i) -> p k i", k=NCD),
                        in_=woT_L).then_inc(SEM["wos"], 16)
            CNT["wos"] += 16
            EV["wo"] = ("wos", CNT["wos"])

            for gidx in range(3):
                d = g.dma_start(out=w2g[gidx][:, :].rearrange("p (f e) -> p f e", f=4),
                                in_=scr_w2_G[gidx])
                d.then_inc(SEM[f"g{gidx}"], 16)
                CNT[f"g{gidx}"] += 16
                EV[f"w2g{gidx}"] = (f"g{gidx}", CNT[f"g{gidx}"])

            wt(g, f"r{NTA-1}m{NCD-1}", f"str{NTA-1}", f"str{NTA-2}")
            for k in range(8):
                d = g.dma_start(out=w1_s[:, k * 4096:(k + 1) * 4096],
                                in_=w1n[:, k * 4096:(k + 1) * 4096])
                d.then_inc(SEM["w1s"], 16)
                CNT["w1s"] += 16
                EV[f"w1c{k}"] = ("w1s", CNT["w1s"])


        def b_loads_x(j):
            if j >= NTB:
                return

            @block.sync
            def _(sync, j=j):
                p3 = j % 3
                if j == 0:
                    wt(sync, *A_EARLY)
                wt(sync, f"y{j-3}")
                sbx = f"bx{p3}"
                d = sync.dma_start(out=xB[p3][:, :].rearrange("p (c t) -> p c t", t=TB),
                                   in_=xin_B[j // 2, j % 2])
                d.then_inc(SEM[sbx], 16)
                CNT[sbx] += 16
                EV[f"Bxr{j}"] = (sbx, CNT[sbx])

        def b_loads_wu(j):
            if j >= NTB:
                return

            def issue(eng, j, with_u):
                par = j % 2
                sbs = f"bs{par}"
                eng.dma_start(out=wpB[par][:, :].rearrange("p (c t) -> p c t", t=TB),
                              in_=scr_w_B[j // 2, j % 2]).then_inc(SEM[sbs], 16)
                CNT[sbs] += 16
                EV[f"Bwuw{j}"] = (sbs, CNT[sbs])
                if with_u:
                    issue_u(eng, j)
                d = eng.dma_start(out=rB[par][:, :].rearrange("p (c t) -> p c t", t=TB),
                                  in_=scr_r_B[j // 2, j % 2])
                d.then_inc(SEM[sbs], 16)
                CNT[sbs] += 16
                EV[f"Bwur{j}"] = (sbs, CNT[sbs])

            def issue_u(eng, j):
                wt(eng, f"bscan{j-1}")
                sbs = f"bs{j % 2}"
                eng.dma_start(out=uB[:, :].rearrange("p (c t) -> p c t", t=TB),
                              in_=scr_u_B[j // 2, j % 2]).then_inc(SEM[sbs], 16)
                CNT[sbs] += 16
                EV[f"Bwuu{j}"] = (sbs, CNT[sbs])

            if j < 2:
                @block.sync
                def _(sync, j=j):
                    wt(sync, *A_EARLY)
                    wt(sync, f"pr{NTA-1}m{NCD-1}")
                    issue(sync, j, with_u=(j == 0))
            else:
                @block.scalar
                def _(sc, j=j):
                    wt(sc, f"batn{j-2}", f"bxn2{j-2}", f"bs2{j-2}")
                    issue(sc, j, with_u=True)

            if j == 1:
                LATE_U.append(issue_u)

        def b_scan(j):
            @block.vector
            def _(v, j=j):
                v = _VSer(v)
                par = j % 2
                if j == 0:
                    wt(v, "cB")
                wt(v, f"Bwuw{j}", f"Bwuu{j}")
                for c in range(NCD):
                    sl = slice(c * TB, (c + 1) * TB)
                    v.tensor_tensor_scan(sB[:, :], wpB[par][:, sl], uB[:, sl],
                                         carryB[:, c:c + 1], op0=ALU.mult, op1=ALU.add)
                    v.tensor_copy(carryB[:, c:c + 1], sB[:, TB - 1:TB])
                    if c == NCD - 1:
                        vev(f"bscan{j}")
                    if c == 0:
                        wt(v, f"Bwur{j}")
                    v.tensor_tensor(wpB[par][:, sl], sB[:, :], rB[par][:, sl], op=ALU.mult)
                vev(f"bout{j}")

        def b_sq(j):
            @block.scalar
            def _(sc, j=j):
                par = j % 2
                wt(sc, f"bout{j}", f"f1{j-2}h{NHC-1}")
                inc(sc.activation(sh[par][:, :], wpB[par][:, :], AF.Square),
                    "ssem", 1, f"bsq{j}")

        def t_brs(t, j):
            par = j % 2
            wt(t, f"bsq{j}", f"brow2{j-1}")
            li = None
            for ki in range(NCD):
                li = t.matmul(PS1[0:1, 0:TB], ones_b[:, 0:1], sh[par][:, ki * TB:(ki + 1) * TB],
                              start=(ki == 0), stop=(ki == NCD - 1))
            inc(li, "tsem", 1, f"brs{j}")

        def b_rows(j):
            @block.vector
            def _(v, j=j):
                v = _VSer(v)
                wt(v, f"brs{j}", f"bsqr{j-1}")
                v.tensor_scalar(rowB[0:1, RB_MSR:RB_MSR + TB], PS1[0:1, 0:TB], INV_D, EPS_RMS,
                                op0=ALU.mult, op1=ALU.add)
                v.reciprocal(rowB[0:1, RB_REC:RB_REC + TB], rowB[0:1, RB_MSR:RB_MSR + TB])
                vev(f"brow{j}")

        def b_sqrt(j):
            @block.scalar
            def _(sc, j=j):
                wt(sc, f"brow{j}", f"bbc{j-1}")
                inc(sc.activation(rowB[0:1, RB_RSTD:RB_RSTD + TB],
                                  rowB[0:1, RB_REC:RB_REC + TB], AF.Sqrt),
                    "ssem", 1, f"bsqr{j}")

        def t_bbc(t, j):
            wt(t, f"bsqr{j}", f"bevac2{j-1}")
            inc(t.matmul(PS2[:, 0:TB], ones_f[0:1, :], rowB[0:1, RB_RSTD:RB_RSTD + TB],
                         start=True, stop=True), "tsem", 1, f"bbc{j}")

        def b_evacr(j):
            @block.vector
            def _(v, j=j):
                v = _VSer(v)
                par = j % 2
                wt(v, f"bbc{j}")
                v.tensor_copy(rstdB_s[:, :], PS2[:, 0:TB])
                vev(f"bevacr{j}")

        def b_attn(j):
            @block.vector
            def _(v, j=j):
                v = _VSer(v)
                par = j % 2
                wt(v, f"bevacr{j}", f"brs{j}")
                for c in range(NCD):
                    sl = slice(c * TB, (c + 1) * TB)
                    v.tensor_tensor(sh[par][:, sl], wpB[par][:, sl],
                                    rstdB_s[:, :], op=ALU.mult)
                vev(f"batn{j}")

        def t_wo(t, j, mi):
            par = j % 2
            if mi == 0:
                wt(t, "wo", f"batn{j}")
            if j == 0 and mi < 2:
                wt(t, f"r{NTA-1}m{NCD-1}")
            if mi >= 4:
                wt(t, f"bwep{j}p{mi//2 - 2}")
            elif mi < 2:
                wt(t, f"bwep{j-1}p{2 + mi}")
            bank = PP[(mi // 2) % 2]
            half = mi % 2
            li = None
            for ki in range(NCD):
                li = t.matmul(bank[:, half * TB: half * TB + TB],
                              woT_s[:, ki * D + mi * 128: ki * D + mi * 128 + 128],
                              sh[par][:, ki * TB:(ki + 1) * TB],
                              start=(ki == 0), stop=(ki == NCD - 1),
                              skip_group_check=True)
            inc(li, "tsem", 1, f"bwo{j}m{mi}")

        def b_bwe_pair(j, p):
            @block.vector
            def _(v, j=j, p=p):
                v = _VSer(v)
                p3 = j % 3
                wt(v, f"bwo{j}m{2*p+1}")
                if p == 0:
                    wt(v, f"Bxr{j}")
                v.tensor_tensor(xB[p3][:, 2 * p * TB: (2 * p + 2) * TB],
                                xB[p3][:, 2 * p * TB: (2 * p + 2) * TB],
                                PP[p % 2][:, 0:2 * TB], op=ALU.add)
                vev(f"bwep{j}p{p}")

        def b_xb2sq2(j):
            @block.vector
            def _(v, j=j):
                v = _VSer(v)
                par = j % 2
                p3 = j % 3
                v.tensor_copy(rB[par][:, :], xB[p3][:, :])
                vev(f"bxb2{j}")

            @block.scalar
            def _(sc, j=j):
                par = j % 2
                p3 = j % 3
                wt(sc, f"bwep{j}p3")
                inc(sc.activation(sh[par][:, :], xB[p3][:, :], AF.Square),
                    "ssem", 1, f"bsq2{j}")

        def t_stats2(t, j):
            par = j % 2
            wt(t, f"bxb2{j}", f"bsq2{j}", f"brow{j}")
            for ki in range(NCD):
                t.matmul(PS1[0:1, 0:TB], ones_b[:, 0:1], rB[par][:, ki * TB:(ki + 1) * TB],
                         start=(ki == 0), stop=(ki == NCD - 1), skip_group_check=True)
            li = None
            for ki in range(NCD):
                li = t.matmul(PS1[0:1, TB:2 * TB], ones_b[:, 0:1],
                              sh[par][:, ki * TB:(ki + 1) * TB],
                              start=(ki == 0), stop=(ki == NCD - 1), skip_group_check=True)
            inc(li, "tsem", 1, f"bs2{j}")

        def b_rows2(j):
            @block.vector
            def _(v, j=j):
                v = _VSer(v)
                wt(v, f"bs2{j}", f"bsqr2{j-1}")
                v.tensor_scalar(rowB[0:1, RB_M2:RB_M2 + TB], PS1[0:1, 0:TB], INV_D, None,
                                op0=ALU.mult)
                v.tensor_scalar(rowB[0:1, RB_E2:RB_E2 + TB], PS1[0:1, TB:2 * TB], INV_D, EPS_LN,
                                op0=ALU.mult, op1=ALU.add)
                v.tensor_tensor(rowB[0:1, RB_T1:RB_T1 + TB], rowB[0:1, RB_M2:RB_M2 + TB],
                                rowB[0:1, RB_M2:RB_M2 + TB], op=ALU.mult)
                v.tensor_tensor(rowB[0:1, RB_T2:RB_T2 + TB], rowB[0:1, RB_E2:RB_E2 + TB],
                                rowB[0:1, RB_T1:RB_T1 + TB], op=ALU.subtract)
                v.reciprocal(rowB[0:1, RB_T1:RB_T1 + TB], rowB[0:1, RB_T2:RB_T2 + TB])
                vev(f"brow2{j}")

        def b_sqrt2(j):
            @block.scalar
            def _(sc, j=j):
                wt(sc, f"brow2{j}", f"bbc2{j-1}")
                inc(sc.activation(rowB[0:1, RB_RSTD2:RB_RSTD2 + TB],
                                  rowB[0:1, RB_T1:RB_T1 + TB], AF.Sqrt),
                    "ssem", 1, f"bsqr2{j}")

        def t_bcast2(t, j):
            wt(t, f"bsqr2{j}", f"bevacr{j}")
            t.matmul(PS2[:, 0:TB], ones_f[0:1, :], rowB[0:1, RB_M2:RB_M2 + TB],
                     start=True, stop=True, skip_group_check=True)
            inc(t.matmul(PS2[:, TB:2 * TB], ones_f[0:1, :],
                         rowB[0:1, RB_RSTD2:RB_RSTD2 + TB],
                         start=True, stop=True, skip_group_check=True),
                "tsem", 1, f"bbc2{j}")

        def b_evac2(j):
            @block.vector
            def _(v, j=j):
                v = _VSer(v)
                par = j % 2
                wt(v, f"bbc2{j}")
                v.tensor_copy(m2B_s[:, :], PS2[:, 0:TB])
                v.tensor_copy(rstd2B_s[:, :], PS2[:, TB:2 * TB])
                vev(f"bevac2{j}")

        def b_xn2(j):
            @block.vector
            def _(v, j=j):
                v = _VSer(v)
                par = j % 2
                p3 = j % 3
                wt(v, f"bevac2{j}", f"bs2{j}")
                for c in range(NCD):
                    sl = slice(c * TB, (c + 1) * TB)
                    v.tensor_tensor(wpB[par][:, sl], xB[p3][:, sl],
                                    m2B_s[:, :], op=ALU.subtract)
                for c in range(NCD):
                    sl = slice(c * TB, (c + 1) * TB)
                    v.tensor_tensor(sh[par][:, sl], wpB[par][:, sl],
                                    rstd2B_s[:, :], op=ALU.mult)
                vev(f"bxn2{j}")

        def b_w2g(gidx):
            @block.gpsimd
            def _(g, gidx=gidx):
                gp = gidx % 3
                wt(g, f"gfree{gidx-3}")
                d = g.dma_start(out=w2g[gp][:, :].rearrange("p (f e) -> p f e", f=4),
                                in_=scr_w2_G[gidx % 8])
                d.then_inc(SEM[f"g{gp}"], 16)
                CNT[f"g{gp}"] += 16
                EV[f"w2g{gidx}"] = (f"g{gp}", CNT[f"g{gp}"])

        FB = [PK1, PK2, PB1, PB2]

        def t_f1(t, j, hi):
            if hi == 0:
                wt(t, f"bxn2{j}", f"bfePKa{j-1}")
            if hi == 1:
                wt(t, f"bfePK{j-1}")
            if hi == 2:
                wt(t, f"bfePB{j-1}")
            if j == 0 and hi % 4 == 0:
                wt(t, f"w1c{hi // 4}")
            f1bank = FB[hi % 4]
            wt(t, f"g{j}h{hi-4}")
            par = j % 2
            li = None
            for ki in range(NCD):
                li = t.matmul(f1bank[:, 0:TB],
                              w1_s[:, hi * NCD * 128 + ki * 128: hi * NCD * 128 + ki * 128 + 128],
                              sh[par][:, ki * TB:(ki + 1) * TB],
                              start=(ki == 0), stop=(ki == NCD - 1),
                              skip_group_check=True)
            inc(li, "tsem", 1, f"f1{j}h{hi}")

        def s_gelu(j, hi):
            @block.scalar
            def _(sc, j=j, hi=hi):
                wt(sc, f"f1{j}h{hi}")
                f1bank = FB[hi % 4]
                inc(sc.activation(h_s[:, hi * TB:(hi + 1) * TB], f1bank[:, 0:TB], AF.Gelu,
                                  bias=b1_s[:, hi:hi + 1]), "ssem", 1, f"g{j}h{hi}")

        def t_f2open(t, j):
            wt(t, f"g{j}h{NHC-1}", "consts2", "carry0")
            for mi in range(NCD):
                bank = FB[mi // 2]
                half = mi % 2
                t.matmul(bank[:, half * TB: half * TB + TB],
                         b2b_s[0:1, mi * 128:(mi + 1) * 128], ones_r[0:1, :],
                         start=(half == 0), stop=False, skip_group_check=True)

        def t_f2(t, j, hi):
            gidx = j * 8 + hi // 4
            if hi % 4 == 0:
                wt(t, f"w2g{gidx}")
            gp = gidx % 3
            li = None
            for mi in range(NCD):
                bank = FB[mi // 2]
                half = mi % 2
                li = t.matmul(bank[:, half * TB: half * TB + TB],
                              w2g[gp][:, (hi % 4) * D + mi * 128: (hi % 4) * D + mi * 128 + 128],
                              h_s[:, hi * TB:(hi + 1) * TB],
                              start=False, stop=(hi == NHC - 1),
                              skip_group_check=True)
            inc(li, "tsem", 1, f"f2{j}h{hi}")
            if hi % 4 == 3:
                EV[f"gfree{gidx}"] = EV[f"f2{j}h{hi}"]

        def t_f2last(t, j):
            gidx = j * 8 + 7
            wt(t, f"w2g{gidx}")
            gp = gidx % 3
            li = None
            for mi in range(NCD):
                bank = FB[mi // 2]
                half = mi % 2
                for hi in range(NHC - 4, NHC):
                    li = t.matmul(bank[:, half * TB: half * TB + TB],
                                  w2g[gp][:, (hi % 4) * D + mi * 128: (hi % 4) * D + mi * 128 + 128],
                                  h_s[:, hi * TB:(hi + 1) * TB],
                                  start=False, stop=(hi == NHC - 1),
                                  skip_group_check=True)
                if mi == 1:
                    inc(li, "tsem", 1, f"f2PKa{j}")
                if mi == 3:
                    inc(li, "tsem", 1, f"f2PKb{j}")
            inc(li, "tsem", 1, f"f2{j}h{NHC-1}")
            EV[f"gfree{gidx}"] = EV[f"f2{j}h{NHC-1}"]

        def b_bfePK(j):
            @block.vector
            def _(v, j=j):
                v = _VSer(v)
                p3 = j % 3
                wt(v, f"f2PKa{j}")
                v.tensor_tensor(xB[p3][:, 0:2 * TB], xB[p3][:, 0:2 * TB],
                                PK1[:, 0:2 * TB], op=ALU.add)
                vev(f"bfePKa{j}")
                wt(v, f"f2PKb{j}")
                v.tensor_tensor(xB[p3][:, 2 * TB:4 * TB], xB[p3][:, 2 * TB:4 * TB],
                                PK2[:, 0:2 * TB], op=ALU.add)
                vev(f"bfePK{j}")

        def b_bfePB(j):
            @block.vector
            def _(v, j=j):
                v = _VSer(v)
                p3 = j % 3
                wt(v, f"f2{j}h{NHC-1}")
                v.tensor_tensor(xB[p3][:, 4 * TB:6 * TB], xB[p3][:, 4 * TB:6 * TB],
                                PB1[:, 0:2 * TB], op=ALU.add)
                v.tensor_tensor(xB[p3][:, 6 * TB:8 * TB], xB[p3][:, 6 * TB:8 * TB],
                                PB2[:, 0:2 * TB], op=ALU.add)
                vev(f"bfePB{j}")

        def b_ystore(j):
            @block.sync
            def _(sync, j=j):
                par = j % 2
                p3 = j % 3
                wt(sync, f"bfePB{j}", f"bfePK{j}")
                sy = f"y{par}"
                d = sync.dma_start(out=yout[j], in_=xB[p3][:, :])
                d.then_inc(SEM[sy], 16)
                CNT[sy] += 16
                EV[f"y{j}"] = (sy, CNT[sy])

        LATE_U = []
        b_loads_wu(0)
        b_loads_x(0)
        b_loads_wu(1)

        @block.sync
        def _(sync):
            wt(sync, "ccdone")
            d = sync.dma_start(out=G_s[:, :], in_=cc_out)
            d.then_inc(SEM["gss"], 16)
            CNT["gss"] += 16
            EV["Gs"] = ("gss", CNT["gss"])

        @block.tensor
        def _(t):
            wt(t, "Gs", f"xn{NTA-1}")
            li = None
            for c in range(NCD):
                li = t.matmul(PB1[:, c:c + 1], G_s[:, c * 128:(c + 1) * 128], sel_s[:, :],
                              start=True, stop=True, skip_group_check=True)
            inc(li, "tsem", 1, "sel")

        @block.vector
        def _(v):
            v = _VSer(v)
            wt(v, "sel")
            v.tensor_copy(carryB[:, :], PB1[:, 0:NCD])
            vev("cB")

        b_loads_x(1)
        b_scan(0)

        @block.scalar
        def _(sc):
            LATE_U[0](sc, 1)

        b_scan(1)
        b_sq(0)
        b_sq(1)

        @block.tensor
        def _(t):
            t_brs(t, 0)
        b_rows(0)
        b_sqrt(0)

        @block.tensor
        def _(t):
            t_bbc(t, 0)
        b_evacr(0)
        b_attn(0)
        for mi in range(NCD):
            @block.tensor
            def _(t, mi=mi):
                t_wo(t, 0, mi)
            if mi % 2 == 1:
                b_bwe_pair(0, mi // 2)
        b_xb2sq2(0)

        @block.tensor
        def _(t):
            t_stats2(t, 0)
        b_rows2(0)
        b_sqrt2(0)

        @block.tensor
        def _(t):
            t_bcast2(t, 0)
        b_evac2(0)
        b_xn2(0)

        @block.tensor
        def _(t):
            t_brs(t, 1)
        b_rows(1)
        b_sqrt(1)

        @block.tensor
        def _(t):
            t_bbc(t, 1)
        b_evacr(1)
        b_attn(1)

        for j in range(NTB):
            cj = j + 1
            c2 = j + 2

            if c2 < NTB:
                b_loads_wu(c2)
            b_loads_x(j + 2)

            for hi in range(NHC):
                @block.tensor
                def _(t, j=j, hi=hi):
                    t_f1(t, j, hi)
                s_gelu(j, hi)

                if cj < NTB:
                    if 2 <= hi <= 9:
                        @block.tensor
                        def _(t, cj=cj, mi=hi - 2):
                            t_wo(t, cj, mi)
                        if (hi - 2) % 2 == 1:
                            b_bwe_pair(cj, (hi - 2) // 2)
                    if hi == 10:
                        b_xb2sq2(cj)
                    if hi == 13:
                        @block.tensor
                        def _(t, cj=cj):
                            t_stats2(t, cj)
                        b_rows2(cj)
                    if hi == 16:
                        b_sqrt2(cj)
                    if hi == 18:
                        @block.tensor
                        def _(t, cj=cj):
                            t_bcast2(t, cj)
                        b_evac2(cj)
                        b_xn2(cj)

            @block.tensor
            def _(t, j=j):
                t_f2open(t, j)
            for g in range(8):
                @block.tensor
                def _(t, j=j, g=g):
                    if g == 7:
                        t_f2last(t, j)
                    else:
                        for hi in range(4 * g, 4 * g + 4):
                            t_f2(t, j, hi)
                if g + 3 < 8:
                    b_w2g(j * 8 + g + 3)
                if c2 < NTB:
                    if g == 0:
                        b_scan(c2)
                    if g == 1:
                        b_sq(c2)
                    if g == 4:
                        @block.tensor
                        def _(t, c2=c2):
                            t_brs(t, c2)
                        b_rows(c2)
                        b_sqrt(c2)
                    if g == 6:
                        @block.tensor
                        def _(t, c2=c2):
                            t_bbc(t, c2)
                        b_evacr(c2)
                        b_attn(c2)

            if j + 1 < NTB:
                for gg in range(3):
                    b_w2g((j + 1) * 8 + gg)

            b_bfePK(j)
            b_bfePB(j)
            b_ystore(j)

        @block.sync
        def _(sync):
            wt(sync, f"y{NTB-1}", f"y{NTB-2}")

        fake = _FakeEng()
        for _name, _fn in PROG:
            _fn(fake)

        PASS[0] = 2
        for _k in CNT:
            CNT[_k] = 0
        with nc.Block() as realblock:
            for _eng_name in ENGS:
                def _run(eng, _eng_name=_eng_name):
                    for _n2, _fn in PROG:
                        if _n2 == _eng_name:
                            _fn(eng)
                getattr(realblock, _eng_name)(_run)

    return nc


def kernel(L0_attn_x, L0_attn_Wr, L0_attn_Wk, L0_attn_Wv, L0_attn_Ww,
           L0_attn_Wo, L0_ffn_w1, L0_ffn_b1, L0_ffn_w2, L0_ffn_b2):
    x = np.ascontiguousarray(np.asarray(L0_attn_x, dtype=np.float32))
    Wr = np.asarray(L0_attn_Wr, dtype=np.float32)
    Wk = np.asarray(L0_attn_Wk, dtype=np.float32)
    Wv = np.asarray(L0_attn_Wv, dtype=np.float32)
    Ww = np.asarray(L0_attn_Ww, dtype=np.float32)
    Wo = np.asarray(L0_attn_Wo, dtype=np.float32)
    w1 = np.ascontiguousarray(np.asarray(L0_ffn_w1, dtype=np.float32))
    b1 = np.asarray(L0_ffn_b1, dtype=np.float32)
    w2 = np.ascontiguousarray(np.asarray(L0_ffn_w2, dtype=np.float32))
    b2 = np.asarray(L0_ffn_b2, dtype=np.float32)

    wrT = np.ascontiguousarray(Wr.T)
    wvT = np.ascontiguousarray(Wv.T)
    wwT = np.ascontiguousarray(Ww.T)
    woT = np.ascontiguousarray(Wo.T)
    wkN = np.ascontiguousarray(Wk)
    w1n = np.ascontiguousarray(
        w1.reshape(NCD, 128, NHC, 128).transpose(1, 2, 0, 3).reshape(128, NHC * NCD * 128))
    b1r = np.ascontiguousarray(b1.reshape(NHC, 128).T)
    b2row = np.ascontiguousarray(b2.reshape(1, D))
    dd = np.arange(D)
    emT = np.ascontiguousarray((dd[:, None] // HD == np.arange(NH)[None, :]).astype(np.float32))
    emask = np.ascontiguousarray(emT.T)
    onesf = np.ones((128, 128), np.float32)

    in_maps = []
    for c in range(NCORES):
        b, h = c // 2, c % 2
        xs = x[b, h * TC:(h + 1) * TC, :]
        xin = np.ascontiguousarray(
            xs.reshape(NTA, TA, NCD, 128).transpose(0, 3, 2, 1).reshape(NTA, 128, NCD * TA))
        sel = np.zeros((2, 1), np.float32)
        if h == 1:
            sel[0, 0] = 1.0
        in_maps.append({
            "xin": xin, "wrT": wrT, "wvT": wvT, "wwT": wwT, "woT": woT,
            "wkN": wkN, "w1n": w1n, "w2": w2, "b1r": b1r, "b2row": b2row,
            "emT": emT, "emask": emask, "onesf": onesf, "selv": sel,
        })

    nc = build_nc()
    res = run_bass_kernel_spmd(nc, in_maps, core_ids=list(range(NCORES)))

    out = np.empty((B, T, D), np.float32)
    for c in range(NCORES):
        b, h = c // 2, c % 2
        yc = np.asarray(res.results[c]["yout"])
        blk = yc.reshape(NTB, 128, NCD, TB).transpose(0, 3, 2, 1).reshape(TC, D)
        out[b, h * TC:(h + 1) * TC, :] = blk
    return out


# version 19
# speedup vs baseline: 1.1501x; 1.1103x over previous
import contextlib
import re

import numpy as np

import concourse.bass as bass
import concourse.mybir as mybir
from concourse.bass_utils import run_bass_kernel_spmd

F32 = mybir.dt.float32
BF16 = mybir.dt.bfloat16
AF = mybir.ActivationFunctionType
ALU = mybir.AluOpType

B, T, D = 4, 8192, 1024
NH, HD = 16, 64
NCORES = 8
TC = T // 2
NCD = 8
TA = 512
NTA = TC // TA
TB = 256
NTB = TC // TB
DFF = 4096
NHC = DFF // 128
EPS_LN = 1e-5
EPS_RMS = float(np.finfo(np.float32).eps)
INV_D = 1.0 / D

RA_M, RA_E2, RA_T1, RA_T2 = 0, 512, 1024, 1536
RB_MSR, RB_REC, RB_RSTD = 0, 256, 512
RB_M2, RB_E2, RB_T1, RB_T2, RB_RSTD2 = 768, 1024, 1280, 1536, 1792


ENGS = ("sync", "gpsimd", "scalar", "tensor", "vector")


class _FakeInstr:
    __slots__ = ()

    def then_inc(self, sem, n):
        return self


class _FakeEng:
    __slots__ = ()
    _FI = _FakeInstr()

    def __getattr__(self, name):
        def f(*a, **k):
            return _FakeEng._FI
        return f


def build_nc():
    nc = bass.Bass("TRN2", target_bir_lowering=False, detect_race_conditions=False)

    xin = nc.dram_tensor("xin", [NTA, 128, NCD * TA], F32, kind="ExternalInput").ap()
    wrT = nc.dram_tensor("wrT", [D, D], F32, kind="ExternalInput").ap()
    wvT = nc.dram_tensor("wvT", [D, D], F32, kind="ExternalInput").ap()
    wwT = nc.dram_tensor("wwT", [D, D], F32, kind="ExternalInput").ap()
    woT = nc.dram_tensor("woT", [D, D], F32, kind="ExternalInput").ap()
    wkN = nc.dram_tensor("wkN", [D, D], F32, kind="ExternalInput").ap()
    w1n = nc.dram_tensor("w1n", [128, NHC * NCD * 128], F32, kind="ExternalInput").ap()
    w2 = nc.dram_tensor("w2", [DFF, D], F32, kind="ExternalInput").ap()
    b1r = nc.dram_tensor("b1r", [128, NHC], F32, kind="ExternalInput").ap()
    b2row = nc.dram_tensor("b2row", [1, D], F32, kind="ExternalInput").ap()
    emT = nc.dram_tensor("emT", [D, NH], F32, kind="ExternalInput").ap()
    emask = nc.dram_tensor("emask", [NH, D], F32, kind="ExternalInput").ap()
    onesf = nc.dram_tensor("onesf", [128, 128], F32, kind="ExternalInput").ap()
    selv = nc.dram_tensor("selv", [2, 1], F32, kind="ExternalInput").ap()

    yout = nc.dram_tensor("yout", [NTB, 128, NCD * TB], F32, kind="ExternalOutput").ap()

    scr_w = nc.dram_tensor("scr_w", [NTA, 128, NCD * TA], F32).ap()
    scr_u = nc.dram_tensor("scr_u", [NTA, 128, NCD * TA], F32).ap()
    scr_r = nc.dram_tensor("scr_r", [NTA, 128, NCD * TA], BF16).ap()
    scr_w2 = nc.dram_tensor("scr_w2", [NHC, 128, D], BF16).ap()
    cc_in = nc.dram_tensor("cc_in", [1, D], F32).ap()
    cc_out = nc.dram_tensor("cc_out", [2, D], F32).ap()

    xin_B = xin.rearrange("n p (c two t) -> n two p c t", two=2, t=TB)
    scr_w_B = scr_w.rearrange("n p (c two t) -> n two p c t", two=2, t=TB)
    scr_u_B = scr_u.rearrange("n p (c two t) -> n two p c t", two=2, t=TB)
    scr_r_B = scr_r.rearrange("n p (c two t) -> n two p c t", two=2, t=TB)
    scr_w2_G = scr_w2.rearrange("(g f) p e -> g p f e", f=4)
    wrT_L = wrT.rearrange("(k p) i -> p k i", p=128)
    wvT_L = wvT.rearrange("(k p) i -> p k i", p=128)
    wwT_L = wwT.rearrange("(k p) i -> p k i", p=128)
    woT_L = woT.rearrange("(k p) i -> p k i", p=128)
    wkN_L = wkN.rearrange("(k p) e -> p k e", p=128)
    emT_L = emT.rearrange("(k p) h -> p k h", p=128)

    es = contextlib.ExitStack()
    with es:
        ones_f = es.enter_context(nc.sbuf_tensor("ones_f", [128, 128], F32))
        ones_b = es.enter_context(nc.sbuf_tensor("ones_b", [128, 128], BF16))
        sel_s = es.enter_context(nc.sbuf_tensor("sel_s", [2, 1], F32))
        b1_s = es.enter_context(nc.sbuf_tensor("b1_s", [128, NHC], F32))
        b2b_s = es.enter_context(nc.sbuf_tensor("b2b_s", [1, D], BF16))
        ones_r = es.enter_context(nc.sbuf_tensor("ones_r", [1, TB], BF16))
        G_s = es.enter_context(nc.sbuf_tensor("G_s", [2, D], F32))
        carryA = es.enter_context(nc.sbuf_tensor("carryA", [128, NCD], F32))
        carryB = es.enter_context(nc.sbuf_tensor("carryB", [128, NCD], F32))
        rowA = es.enter_context(nc.sbuf_tensor("rowA", [1, 2048], F32))
        rowB = rowA


        PS1 = es.enter_context(nc.psum_tensor("PS1", [128, 512], F32))
        PS2 = es.enter_context(nc.psum_tensor("PS2", [128, 512], F32))
        PB1 = es.enter_context(nc.psum_tensor("PB1", [128, 512], F32))
        PB2 = es.enter_context(nc.psum_tensor("PB2", [128, 512], F32))
        PP = [es.enter_context(nc.psum_tensor(f"PP{k}", [128, 512], F32)) for k in range(2)]
        PK1 = es.enter_context(nc.psum_tensor("PK1", [128, 512], F32))
        PK2 = es.enter_context(nc.psum_tensor("PK2", [128, 512], F32))

        SEMN = ["tsem", "vsem", "ssem", "csem", "wt", "wtg", "wtw", "xb", "cc", "y",
                "wk0", "wk1", "w1s", "wos", "gss", "y0", "y1",
                "x0", "x1", "st0", "st1",
                "w2l0", "w2l1", "w2l2", "w2l3", "w2s0", "w2s1", "w2s2", "w2s3",
                "bx0", "bx1", "bx2", "bs0", "bs1", "g0", "g1", "g2"]
        SEM = {n: es.enter_context(nc.semaphore(n)) for n in SEMN}
        CNT = {n: 0 for n in SEMN}
        EV = {}
        PASS = [1]

        def inc(instr, s, n, evname=None):
            CNT[s] += n
            instr.then_inc(SEM[s], n)
            if evname is not None:
                if evname in EV and PASS[0] == 2:
                    assert EV[evname] == (s, CNT[s]), \
                        f"event {evname} diverged between passes"
                EV[evname] = (s, CNT[s])
            return instr

        def wt(eng, *names):
            for name in names:
                if name in EV:
                    s, v = EV[name]
                    eng.wait_ge(SEM[s], v)
                else:
                    assert name == "none" or re.search(r"-\d", name), \
                        f"wait on event {name!r} that was never recorded"

        _VOPS = ("tensor_tensor", "tensor_scalar", "tensor_copy", "reciprocal",
                 "tensor_tensor_scan", "memset", "scalar_tensor_tensor")

        class _VSer:
            __slots__ = ("_v",)

            def __init__(self, v):
                self._v = v

            def __getattr__(self, name):
                f = getattr(self._v, name)
                if name in _VOPS:
                    def wrapped(*a, **k):
                        self._v.wait_ge(SEM["vsem"], CNT["vsem"])
                        r = f(*a, **k)
                        CNT["vsem"] += 1
                        r.then_inc(SEM["vsem"], 1)
                        return r
                    return wrapped
                return f

        def vev(*names):
            for name in names:
                if name in EV and PASS[0] == 2:
                    assert EV[name] == ("vsem", CNT["vsem"]), \
                        f"event {name} diverged between passes"
                EV[name] = ("vsem", CNT["vsem"])

        PROG = []

        class _Sec:
            def __getattr__(self, eng_name):
                def reg(fn):
                    PROG.append((eng_name, fn))
                return reg

        block = _Sec()

        @block.sync
        def _(sync):
            sync.dma_start(out=ones_f[:, :], in_=onesf).then_inc(SEM["wt"], 16)
            sync.dma_start(out=sel_s[:, :], in_=selv).then_inc(SEM["wt"], 16)
            sync.dma_start(out=b1_s[:, :], in_=b1r).then_inc(SEM["wt"], 16)
            CNT["wt"] += 48
            EV["consts"] = ("wt", CNT["wt"])

        @block.gpsimd
        def _(g):
            g.dma_start(out=ones_b[:, :], in_=onesf).then_inc(SEM["wtg"], 16)
            g.dma_start(out=emask_s[:, :], in_=emask).then_inc(SEM["wtg"], 16)
            g.dma_start(out=emT_s[:, :].rearrange("p (k h) -> p k h", k=NCD),
                        in_=emT_L).then_inc(SEM["wtg"], 16)
            g.dma_start(out=b2b_s[:, :], in_=b2row).then_inc(SEM["wtg"], 16)
            CNT["wtg"] += 64
            EV["consts2"] = ("wtg", CNT["wtg"])

        @block.vector
        def _(v):
            v = _VSer(v)
            v.memset(carryA[:, :], 0.0)
            v.memset(ones_r[:, :], 1.0)
            vev("carry0")

        esA = contextlib.ExitStack()
        K_s = esA.enter_context(nc.sbuf_tensor("K_s", [16, TA], BF16))
        emask_s = esA.enter_context(nc.sbuf_tensor("emask_s", [16, D], BF16))
        emT_s = esA.enter_context(nc.sbuf_tensor("emT_s", [128, NCD * 16], BF16))
        wkhT_s = esA.enter_context(nc.sbuf_tensor("wkhT_s", [128, NCD * 16], BF16))
        WrT_s = esA.enter_context(nc.sbuf_tensor("WrT_s", [128, NCD * D], BF16))
        WvT_s = esA.enter_context(nc.sbuf_tensor("WvT_s", [128, NCD * D], BF16))
        WwT_s = esA.enter_context(nc.sbuf_tensor("WwT_s", [128, NCD * D], BF16))
        wkst = [esA.enter_context(nc.sbuf_tensor(f"wkst{k}", [128, D], BF16)) for k in range(2)]
        w2st = [esA.enter_context(nc.sbuf_tensor(f"w2st{k}", [128, D], BF16)) for k in range(4)]
        xA = [esA.enter_context(nc.sbuf_tensor(f"xA{k}", [128, NCD * TA], F32)) for k in range(2)]
        xbA = esA.enter_context(nc.sbuf_tensor("xbA", [128, NCD * TA], BF16))
        xsqA = esA.enter_context(nc.sbuf_tensor("xsqA", [128, NCD * TA], BF16))
        xnA = [esA.enter_context(nc.sbuf_tensor(f"xnA{k}", [128, NCD * TA], BF16)) for k in range(2)]
        sigw = esA.enter_context(nc.sbuf_tensor("sigw", [128, NCD * TA], F32))
        ubuf = esA.enter_context(nc.sbuf_tensor("ubuf", [128, NCD * TA], F32))
        vbf = esA.enter_context(nc.sbuf_tensor("vbf", [128, NCD * TA], BF16))
        rbf = esA.enter_context(nc.sbuf_tensor("rbf", [128, NCD * TA], BF16))
        sacr = esA.enter_context(nc.sbuf_tensor("sacr", [128, TA], F32))

        @block.gpsimd
        def _(g):
            g.dma_start(out=WrT_s[:, :].rearrange("p (k i) -> p k i", k=NCD),
                        in_=wrT_L).then_inc(SEM["wtw"], 16)
            g.dma_start(out=WvT_s[:, :].rearrange("p (k i) -> p k i", k=NCD),
                        in_=wvT_L).then_inc(SEM["wtw"], 16)
            g.dma_start(out=WwT_s[:, :].rearrange("p (k i) -> p k i", k=NCD),
                        in_=wwT_L).then_inc(SEM["wtw"], 16)
            CNT["wtw"] += 48
            EV["wts"] = ("wtw", CNT["wtw"])

        for ki in range(NCD):
            @block.gpsimd
            def _(g, ki=ki):
                wt(g, f"wkmm{ki-2}")
                sw = f"wk{ki % 2}"
                d = g.dma_start(out=wkst[ki % 2][:, :], in_=wkN_L[:, ki, :])
                d.then_inc(SEM[sw], 16)
                CNT[sw] += 16
                EV[f"wkld{ki}"] = (sw, CNT[sw])

            @block.tensor
            def _(t, ki=ki):
                if ki == 0:
                    wt(t, "consts2", "wts")
                wt(t, f"wkld{ki}")
                li = None
                for mi in range(NCD):
                    li = t.matmul(PK1[:, mi * 16:(mi + 1) * 16],
                                  wkst[ki % 2][:, mi * 128: mi * 128 + 128],
                                  emT_s[:, ki * 16:(ki + 1) * 16],
                                  start=(ki == 0 and mi == 0), stop=(ki == NCD - 1),
                                  skip_group_check=True)
                inc(li, "tsem", 1, f"wkmm{ki}")

        @block.vector
        def _(v):
            v = _VSer(v)
            wt(v, f"wkmm{NCD-1}")
            v.tensor_copy(wkhT_s[:, :], PK1[:, 0:128])
            vev("wkh")


        RQ = {"q": 0}
        qevent = {}

        def a_xload(i):
            if i >= NTA:
                return

            @block.sync
            def _(sync, i=i):
                wt(sync, f"xn{i-2}", f"xsq{i-2}")
                d = sync.dma_start(out=xA[i % 2][:, :], in_=xin[i])
                d.then_inc(SEM[f"x{i % 2}"], 16)
                CNT[f"x{i % 2}"] += 16
                EV[f"xA{i}"] = (f"x{i % 2}", CNT[f"x{i % 2}"])

        def a_xbload(i):
            if i >= NTA:
                return

            @block.gpsimd
            def _(g, i=i):
                wt(g, f"stats{i-1}")
                d = g.dma_start(out=xbA[:, :], in_=xin[i])
                d.then_inc(SEM["xb"], 16)
                CNT["xb"] += 16
                EV[f"xb{i}"] = ("xb", CNT["xb"])

        def a_xsq_half(i, h):
            if i >= NTA:
                return

            @block.vector
            def _(v, i=i, h=h):
                v = _VSer(v)
                if h == 0:
                    wt(v, f"xA{i}", f"stats{i-1}")
                sl = slice(h * 4 * TA, (h + 1) * 4 * TA)
                v.tensor_tensor(xsqA[:, sl], xA[i % 2][:, sl], xA[i % 2][:, sl],
                                op=ALU.mult)
                if h == 1:
                    vev(f"xsq{i}")

        def a_stats(i):
            if i >= NTA:
                return

            @block.tensor
            def _(t, i=i):
                wt(t, f"xb{i}", f"xsq{i}", f"rows{i-1}")
                for ki in range(NCD):
                    t.matmul(PS1[0:1, 0:TA], ones_b[:, 0:1], xbA[:, ki * TA:(ki + 1) * TA],
                             start=(ki == 0), stop=(ki == NCD - 1))
                li = None
                for ki in range(NCD):
                    li = t.matmul(PS2[0:1, 0:TA], ones_b[:, 0:1], xsqA[:, ki * TA:(ki + 1) * TA],
                                  start=(ki == 0), stop=(ki == NCD - 1))
                inc(li, "tsem", 1, f"stats{i}")

        def a_rows(i):
            if i >= NTA:
                return

            @block.vector
            def _(v, i=i):
                v = _VSer(v)
                wt(v, f"stats{i}", f"sqrt{i-1}", f"bcast{i-1}")
                v.tensor_scalar(rowA[0:1, RA_M:RA_M + TA], PS1[0:1, 0:TA], INV_D, None,
                                op0=ALU.mult)
                v.tensor_scalar(rowA[0:1, RA_E2:RA_E2 + TA], PS2[0:1, 0:TA], INV_D, EPS_LN,
                                op0=ALU.mult, op1=ALU.add)
                v.tensor_tensor(rowA[0:1, RA_T1:RA_T1 + TA], rowA[0:1, RA_M:RA_M + TA],
                                rowA[0:1, RA_M:RA_M + TA], op=ALU.mult)
                v.tensor_tensor(rowA[0:1, RA_T2:RA_T2 + TA], rowA[0:1, RA_E2:RA_E2 + TA],
                                rowA[0:1, RA_T1:RA_T1 + TA], op=ALU.subtract)
                v.reciprocal(rowA[0:1, RA_T1:RA_T1 + TA], rowA[0:1, RA_T2:RA_T2 + TA])
                vev(f"rows{i}")

        def a_sqrt(i):
            if i >= NTA:
                return

            @block.scalar
            def _(sc, i=i):
                wt(sc, f"rows{i}")
                inc(sc.activation(rowA[0:1, RA_T2:RA_T2 + TA], rowA[0:1, RA_T1:RA_T1 + TA],
                                  AF.Sqrt), "ssem", 1, f"sqrt{i}")

        def a_bcast(i):
            if i >= NTA:
                return

            @block.tensor
            def _(t, i=i):
                wt(t, "consts", f"sqrt{i}", f"xn{i-1}")
                t.matmul(PB1[:, 0:TA], ones_f[0:1, :], rowA[0:1, RA_M:RA_M + TA],
                         start=True, stop=True)
                li = t.matmul(PB2[:, 0:TA], ones_f[0:1, :], rowA[0:1, RA_T2:RA_T2 + TA],
                              start=True, stop=True)
                inc(li, "tsem", 1, f"bcast{i}")

        def a_xn_half(i, h):
            if i >= NTA:
                return

            @block.vector
            def _(v, i=i, h=h):
                v = _VSer(v)
                if h == 0:
                    wt(v, f"bcast{i}", f"xA{i}", f"pr{i-2}m{NCD-1}")
                for c in range(4 * h, 4 * h + 4):
                    sl = slice(c * TA, (c + 1) * TA)
                    v.tensor_tensor(xA[i % 2][:, sl], xA[i % 2][:, sl], PB1[:, 0:TA],
                                    op=ALU.subtract)
                    v.tensor_tensor(xnA[i % 2][:, sl], xA[i % 2][:, sl], PB2[:, 0:TA],
                                    op=ALU.mult)
                if h == 1:
                    vev(f"xn{i}")

        def a_xn(i):
            a_xn_half(i, 0)
            a_xn_half(i, 1)

        a_xload(0)
        a_xload(1)
        a_xbload(0)
        a_xsq_half(0, 0)
        a_xsq_half(0, 1)
        a_stats(0)
        a_xbload(1)
        a_rows(0)
        a_sqrt(0)
        a_bcast(0)
        a_xn(0)

        def t_kb(t, i, cc):
            wt(t, f"Kev{i}", f"u{i}c{cc-1}" if cc > 0 else f"u{i-1}c{NCD-1}")
            inc(t.matmul(PK2[:, 0:TA], emask_s[:, cc * 128:(cc + 1) * 128],
                         K_s[:, :], start=True, stop=True),
                "tsem", 1, f"kb{i}c{cc}")

        def b_uscan(i, c):
            @block.vector
            def _(v, i=i, c=c):
                v = _VSer(v)
                sl = slice(c * TA, (c + 1) * TA)
                wt(v, f"vb{i}c{c}", f"kb{i}c{c}", f"stu{i-1}")
                v.tensor_tensor(ubuf[:, sl], vbf[:, sl], PK2[:, 0:TA], op=ALU.mult)
                vev(f"u{i}c{c}")
                if c == 0:
                    wt(v, "carry0")
                v.tensor_tensor_scan(sacr[:, :], sigw[:, sl], ubuf[:, sl],
                                     carryA[:, c:c + 1], op0=ALU.mult, op1=ALU.add)
                v.tensor_copy(carryA[:, c:c + 1], sacr[:, TA - 1: TA])
                if c == NCD - 1:
                    vev(f"scan{i}")

        ROTW = [PP[0], PP[1], PK1]

        def emit_r(i, deferred=False):
            sst = f"st{i % 2}"
            for mi in range(NCD):
                bank = ROTW[RQ["q"] % 3]
                wait_prev = qevent.get(RQ["q"] - 3, "none")
                qevent[RQ["q"]] = f"r{i}m{mi}"
                RQ["q"] += 1

                @block.tensor
                def _(t, i=i, mi=mi, bank=bank, wait_prev=wait_prev):
                    wt(t, wait_prev)
                    li = None
                    for ki in range(NCD):
                        li = t.matmul(bank[:, 0:TA],
                                      WrT_s[:, ki * D + mi * 128: ki * D + mi * 128 + 128],
                                      xnA[i % 2][:, ki * TA:(ki + 1) * TA],
                                      start=(ki == 0), stop=(ki == NCD - 1))
                    inc(li, "tsem", 1, f"pr{i}m{mi}")

                @block.scalar
                def _(sc, i=i, mi=mi, bank=bank):
                    wt(sc, f"pr{i}m{mi}", f"str{i-1}")
                    inc(sc.activation(rbf[:, mi * TA:(mi + 1) * TA],
                                      bank[:, 0:TA], AF.Sigmoid),
                        "ssem", 1, f"r{i}m{mi}")

                if deferred:
                    @block.scalar
                    def _(sc, i=i, mi=mi, sst=sst):
                        wt(sc, f"r{i}m{mi}")
                        d = sc.dma_start(out=scr_r[i][:, mi * TA:(mi + 1) * TA],
                                         in_=rbf[:, mi * TA:(mi + 1) * TA])
                        d.then_inc(SEM[sst], 16)
                        CNT[sst] += 16
                        if mi == NCD - 1:
                            EV[f"str{i}"] = (sst, CNT[sst])

        for i in range(NTA):
            par = i % 2
            sst = f"st{par}"

            a_xload(i + 2)

            @block.gpsimd
            def _(g, i=i):
                for k in range(4):
                    s = i * 4 + k
                    sp = s % 4
                    wt(g, f"w2s{s-4}")
                    d = g.dma_start(out=w2st[s % 4][:, :], in_=w2[s * 128:(s + 1) * 128, :])
                    d.then_inc(SEM[f"w2l{sp}"], 16)
                    CNT[f"w2l{sp}"] += 16
                    EV[f"w2l{s}"] = (f"w2l{sp}", CNT[f"w2l{sp}"])

            @block.sync
            def _(sync, i=i):
                for k in range(4):
                    s = i * 4 + k
                    sp = s % 4
                    wt(sync, f"w2l{s}")
                    d = sync.dma_start(out=scr_w2[s], in_=w2st[s % 4][:, :])
                    d.then_inc(SEM[f"w2s{sp}"], 16)
                    CNT[f"w2s{sp}"] += 16
                    EV[f"w2s{s}"] = (f"w2s{sp}", CNT[f"w2s{sp}"])

            @block.tensor
            def _(t, i=i):
                wt(t, "wkh", f"Kev{i-1}", f"u{i-1}c{NCD-1}", f"xn{i}")
                li = None
                for ki in range(NCD):
                    li = t.matmul(PK2[0:16, 0:TA], wkhT_s[:, ki * 16:(ki + 1) * 16],
                                  xnA[i % 2][:, ki * TA:(ki + 1) * TA],
                                  start=(ki == 0), stop=(ki == NCD - 1))
                inc(li, "tsem", 1, f"K{i}")

            @block.scalar
            def _(sc, i=i):
                wt(sc, f"K{i}")
                inc(sc.copy(K_s[:, :], PK2[0:16, 0:TA]), "ssem", 1, f"Kev{i}")

            for c in range(NCD):
                bank = ROTW[RQ["q"] % 3]
                wait_prev = qevent.get(RQ["q"] - 3, "none")
                qevent[RQ["q"]] = f"sgw{i}m{c}"
                RQ["q"] += 1

                @block.tensor
                def _(t, i=i, c=c, bank=bank, wait_prev=wait_prev):
                    if c == 0:
                        wt(t, f"xn{i}")
                    wt(t, wait_prev)
                    if bank is PK1:
                        wt(t, "none")
                    li = None
                    for ki in range(NCD):
                        li = t.matmul(bank[:, 0:TA],
                                      WwT_s[:, ki * D + c * 128: ki * D + c * 128 + 128],
                                      xnA[i % 2][:, ki * TA:(ki + 1) * TA],
                                      start=(ki == 0), stop=(ki == NCD - 1))
                    inc(li, "tsem", 1, f"pw{i}m{c}")

                @block.scalar
                def _(sc, i=i, c=c, bank=bank):
                    wt(sc, f"pw{i}m{c}", f"stw{i-1}", f"scan{i-1}")
                    inc(sc.activation(sigw[:, c * TA:(c + 1) * TA],
                                      bank[:, 0:TA], AF.Sigmoid),
                        "ssem", 1, f"sgw{i}m{c}")

                @block.vector
                def _(v, i=i, c=c):
                    v = _VSer(v)
                    wt(v, f"sgw{i}m{c}")
                    v.tensor_scalar(sigw[:, c * TA:(c + 1) * TA],
                                    sigw[:, c * TA:(c + 1) * TA], 0.99, None,
                                    op0=ALU.mult)
                    vev(f"wpc{i}m{c}")

                bank2 = ROTW[RQ["q"] % 3]
                wait_prev2 = qevent.get(RQ["q"] - 3, "none")
                qevent[RQ["q"]] = f"vb{i}c{c}"
                RQ["q"] += 1

                @block.tensor
                def _(t, i=i, c=c, bank2=bank2, wait_prev2=wait_prev2):
                    wt(t, wait_prev2)
                    li = None
                    for ki in range(NCD):
                        li = t.matmul(bank2[:, 0:TA],
                                      WvT_s[:, ki * D + c * 128: ki * D + c * 128 + 128],
                                      xnA[i % 2][:, ki * TA:(ki + 1) * TA],
                                      start=(ki == 0), stop=(ki == NCD - 1))
                    inc(li, "tsem", 1, f"pv{i}c{c}")
                    t_kb(t, i, c)

                @block.scalar
                def _(sc, i=i, c=c, bank2=bank2):
                    wt(sc, f"pv{i}c{c}", f"u{i-1}c{c}")
                    inc(sc.copy(vbf[:, c * TA:(c + 1) * TA], bank2[:, 0:TA]),
                        "ssem", 1, f"vb{i}c{c}")

                if c == 5:
                    a_rows(i + 1)

                b_uscan(i, c)

                if c == 1:
                    a_xsq_half(i + 1, 0)
                if c == 3:
                    a_xsq_half(i + 1, 1)
                if c == 4:
                    a_stats(i + 1)
                if c == 6:
                    a_sqrt(i + 1)

            a_xbload(i + 2)
            a_bcast(i + 1)
            a_xn_half(i + 1, 0)
            a_xn_half(i + 1, 1)

            if i < NTA - 2:
                emit_r(i)

            @block.sync
            def _(sync, i=i, sst=sst):
                wt(sync, f"wpc{i}m{NCD-1}")
                sync.dma_start(out=scr_w[i], in_=sigw[:, :]).then_inc(SEM[sst], 16)
                CNT[sst] += 16
                wt(sync, f"u{i}c{NCD-1}")
                sync.dma_start(out=scr_u[i], in_=ubuf[:, :]).then_inc(SEM[sst], 16)
                CNT[sst] += 16
                EV[f"stw{i}"] = (sst, CNT[sst])
                EV[f"stu{i}"] = (sst, CNT[sst])
                if i < NTA - 2:
                    wt(sync, f"r{i}m{NCD-1}")
                    sync.dma_start(out=scr_r[i], in_=rbf[:, :]).then_inc(SEM[sst], 16)
                    CNT[sst] += 16
                    EV[f"str{i}"] = (sst, CNT[sst])

        @block.sync
        def _(sync):
            wt(sync, f"scan{NTA-1}")
            for c in range(NCD):
                d = sync.dma_start(out=cc_in[0, c * 128:(c + 1) * 128], in_=carryA[:, c:c + 1])
                d.then_inc(SEM["cc"], 16)
            CNT["cc"] += 16 * NCD
            EV["carry"] = ("cc", CNT["cc"])

        @block.gpsimd
        def _(g):
            wt(g, "carry")
            cci = g.collective_compute(
                "AllGather", ALU.bypass,
                replica_groups=[[2 * p, 2 * p + 1] for p in range(NCORES // 2)],
                ins=[cc_in], outs=[cc_out],
            )
            cci.then_inc(SEM["csem"], 1)
            CNT["csem"] += 1
            EV["ccdone"] = ("csem", CNT["csem"])

        emit_r(NTA - 2, deferred=True)
        emit_r(NTA - 1, deferred=True)

        A_EARLY = [f"pr{NTA-1}m{NCD-1}", f"scan{NTA-1}",
                   f"sgw{NTA-1}m{NCD-1}", f"xn{NTA-1}", f"xn{NTA-2}",
                   f"u{NTA-1}c{NCD-1}", f"stats{NTA-1}", f"xsq{NTA-1}",
                   f"stw{NTA-1}", f"stw{NTA-2}",
                   f"w2s{4*NTA-1}", f"w2s{4*NTA-2}", f"w2s{4*NTA-3}", f"w2s{4*NTA-4}",
                   f"xA{NTA-1}", f"xA{NTA-2}", f"xb{NTA-1}",
                   f"wkld{NCD-1}", f"wkld{NCD-2}"]
        A_DONE = A_EARLY + [f"r{NTA-1}m{NCD-1}", f"str{NTA-1}", f"str{NTA-2}"]

        esA.close()

        w1_s = es.enter_context(nc.sbuf_tensor("w1_s", [128, NCD * DFF], BF16))
        woT_s = es.enter_context(nc.sbuf_tensor("woT_s", [128, NCD * D], BF16))
        w2g = [es.enter_context(nc.sbuf_tensor(f"w2g{k}", [128, 4 * D], BF16)) for k in range(3)]
        xB = [es.enter_context(nc.sbuf_tensor(f"xB{k}", [128, NCD * TB], F32)) for k in range(3)]
        wpB = [es.enter_context(nc.sbuf_tensor(f"wpB{k}", [128, NCD * TB], F32)) for k in range(2)]
        uB = es.enter_context(nc.sbuf_tensor("uB", [128, NCD * TB], F32))
        rB = [es.enter_context(nc.sbuf_tensor(f"rB{k}", [128, NCD * TB], BF16)) for k in range(2)]
        sB = es.enter_context(nc.sbuf_tensor("sB", [128, TB], F32))
        sh = [es.enter_context(nc.sbuf_tensor(f"sh{k}", [128, NCD * TB], BF16)) for k in range(2)]
        h_s = es.enter_context(nc.sbuf_tensor("h_s", [128, NHC * TB], BF16))
        rstdB_s = es.enter_context(nc.sbuf_tensor("rstdB_s", [128, TB], F32))
        m2B_s = es.enter_context(nc.sbuf_tensor("m2B_s", [128, TB], F32))
        rstd2B_s = es.enter_context(nc.sbuf_tensor("rstd2B_s", [128, TB], F32))

        @block.gpsimd
        def _(g):
            wt(g, *A_EARLY)
            g.dma_start(out=woT_s[:, :].rearrange("p (k i) -> p k i", k=NCD),
                        in_=woT_L).then_inc(SEM["wos"], 16)
            CNT["wos"] += 16
            EV["wo"] = ("wos", CNT["wos"])

            for gidx in range(3):
                d = g.dma_start(out=w2g[gidx][:, :].rearrange("p (f e) -> p f e", f=4),
                                in_=scr_w2_G[gidx])
                d.then_inc(SEM[f"g{gidx}"], 16)
                CNT[f"g{gidx}"] += 16
                EV[f"w2g{gidx}"] = (f"g{gidx}", CNT[f"g{gidx}"])

            wt(g, f"r{NTA-1}m{NCD-1}", f"str{NTA-1}", f"str{NTA-2}")
            for k in range(8):
                d = g.dma_start(out=w1_s[:, k * 4096:(k + 1) * 4096],
                                in_=w1n[:, k * 4096:(k + 1) * 4096])
                d.then_inc(SEM["w1s"], 16)
                CNT["w1s"] += 16
                EV[f"w1c{k}"] = ("w1s", CNT["w1s"])


        def b_loads_x(j):
            if j >= NTB:
                return

            @block.sync
            def _(sync, j=j):
                p3 = j % 3
                if j == 0:
                    wt(sync, *A_EARLY)
                wt(sync, f"y{j-3}")
                sbx = f"bx{p3}"
                d = sync.dma_start(out=xB[p3][:, :].rearrange("p (c t) -> p c t", t=TB),
                                   in_=xin_B[j // 2, j % 2])
                d.then_inc(SEM[sbx], 16)
                CNT[sbx] += 16
                EV[f"Bxr{j}"] = (sbx, CNT[sbx])

        def b_loads_wu(j):
            if j >= NTB:
                return

            def issue(eng, j, with_u):
                par = j % 2
                sbs = f"bs{par}"
                eng.dma_start(out=wpB[par][:, :].rearrange("p (c t) -> p c t", t=TB),
                              in_=scr_w_B[j // 2, j % 2]).then_inc(SEM[sbs], 16)
                CNT[sbs] += 16
                EV[f"Bwuw{j}"] = (sbs, CNT[sbs])
                if with_u:
                    issue_u(eng, j)
                d = eng.dma_start(out=rB[par][:, :].rearrange("p (c t) -> p c t", t=TB),
                                  in_=scr_r_B[j // 2, j % 2])
                d.then_inc(SEM[sbs], 16)
                CNT[sbs] += 16
                EV[f"Bwur{j}"] = (sbs, CNT[sbs])

            def issue_u(eng, j):
                wt(eng, f"bscan{j-1}")
                sbs = f"bs{j % 2}"
                eng.dma_start(out=uB[:, :].rearrange("p (c t) -> p c t", t=TB),
                              in_=scr_u_B[j // 2, j % 2]).then_inc(SEM[sbs], 16)
                CNT[sbs] += 16
                EV[f"Bwuu{j}"] = (sbs, CNT[sbs])

            if j < 2:
                @block.sync
                def _(sync, j=j):
                    wt(sync, *A_EARLY)
                    wt(sync, f"pr{NTA-1}m{NCD-1}")
                    issue(sync, j, with_u=(j == 0))
            else:
                @block.scalar
                def _(sc, j=j):
                    wt(sc, f"batn{j-2}", f"bxn2{j-2}", f"bs2{j-2}")
                    issue(sc, j, with_u=True)

            if j == 1:
                LATE_U.append(issue_u)

        def b_scan(j):
            @block.vector
            def _(v, j=j):
                v = _VSer(v)
                par = j % 2
                if j == 0:
                    wt(v, "cB")
                wt(v, f"Bwuw{j}", f"Bwuu{j}")
                for c in range(NCD):
                    sl = slice(c * TB, (c + 1) * TB)
                    v.tensor_tensor_scan(sB[:, :], wpB[par][:, sl], uB[:, sl],
                                         carryB[:, c:c + 1], op0=ALU.mult, op1=ALU.add)
                    v.tensor_copy(carryB[:, c:c + 1], sB[:, TB - 1:TB])
                    if c == NCD - 1:
                        vev(f"bscan{j}")
                    if c == 0:
                        wt(v, f"Bwur{j}")
                    v.tensor_tensor(wpB[par][:, sl], sB[:, :], rB[par][:, sl], op=ALU.mult)
                vev(f"bout{j}")

        def b_sq(j):
            @block.scalar
            def _(sc, j=j):
                par = j % 2
                wt(sc, f"bout{j}", f"f1{j-2}h{NHC-1}")
                inc(sc.activation(sh[par][:, :], wpB[par][:, :], AF.Square),
                    "ssem", 1, f"bsq{j}")

        def t_brs(t, j):
            par = j % 2
            wt(t, f"bsq{j}", f"brow2{j-1}")
            li = None
            for ki in range(NCD):
                li = t.matmul(PS1[0:1, 0:TB], ones_b[:, 0:1], sh[par][:, ki * TB:(ki + 1) * TB],
                              start=(ki == 0), stop=(ki == NCD - 1))
            inc(li, "tsem", 1, f"brs{j}")

        def b_rows(j):
            @block.vector
            def _(v, j=j):
                v = _VSer(v)
                wt(v, f"brs{j}", f"bsqr{j-1}")
                v.tensor_scalar(rowB[0:1, RB_MSR:RB_MSR + TB], PS1[0:1, 0:TB], INV_D, EPS_RMS,
                                op0=ALU.mult, op1=ALU.add)
                v.reciprocal(rowB[0:1, RB_REC:RB_REC + TB], rowB[0:1, RB_MSR:RB_MSR + TB])
                vev(f"brow{j}")

        def b_sqrt(j):
            @block.scalar
            def _(sc, j=j):
                wt(sc, f"brow{j}", f"bbc{j-1}")
                inc(sc.activation(rowB[0:1, RB_RSTD:RB_RSTD + TB],
                                  rowB[0:1, RB_REC:RB_REC + TB], AF.Sqrt),
                    "ssem", 1, f"bsqr{j}")

        def t_bbc(t, j):
            wt(t, f"bsqr{j}", f"bevac2{j-1}")
            inc(t.matmul(PS2[:, 0:TB], ones_f[0:1, :], rowB[0:1, RB_RSTD:RB_RSTD + TB],
                         start=True, stop=True), "tsem", 1, f"bbc{j}")

        def b_evacr(j):
            @block.vector
            def _(v, j=j):
                v = _VSer(v)
                par = j % 2
                wt(v, f"bbc{j}")
                v.tensor_copy(rstdB_s[:, :], PS2[:, 0:TB])
                vev(f"bevacr{j}")

        def b_attn(j):
            @block.vector
            def _(v, j=j):
                v = _VSer(v)
                par = j % 2
                wt(v, f"bevacr{j}", f"brs{j}")
                for c in range(NCD):
                    sl = slice(c * TB, (c + 1) * TB)
                    v.tensor_tensor(sh[par][:, sl], wpB[par][:, sl],
                                    rstdB_s[:, :], op=ALU.mult)
                vev(f"batn{j}")

        def t_wo(t, j, mi):
            par = j % 2
            if mi == 0:
                wt(t, "wo", f"batn{j}")
            if j == 0 and mi < 2:
                wt(t, f"r{NTA-1}m{NCD-1}")
            if mi >= 4:
                wt(t, f"bwep{j}p{mi//2 - 2}")
            elif mi < 2:
                wt(t, f"bwep{j-1}p{2 + mi}")
            bank = PP[(mi // 2) % 2]
            half = mi % 2
            li = None
            for ki in range(NCD):
                li = t.matmul(bank[:, half * TB: half * TB + TB],
                              woT_s[:, ki * D + mi * 128: ki * D + mi * 128 + 128],
                              sh[par][:, ki * TB:(ki + 1) * TB],
                              start=(ki == 0), stop=(ki == NCD - 1),
                              skip_group_check=True)
            inc(li, "tsem", 1, f"bwo{j}m{mi}")

        def b_bwe_pair(j, p):
            @block.vector
            def _(v, j=j, p=p):
                v = _VSer(v)
                p3 = j % 3
                wt(v, f"bwo{j}m{2*p+1}")
                if p == 0:
                    wt(v, f"Bxr{j}")
                v.tensor_tensor(xB[p3][:, 2 * p * TB: (2 * p + 2) * TB],
                                xB[p3][:, 2 * p * TB: (2 * p + 2) * TB],
                                PP[p % 2][:, 0:2 * TB], op=ALU.add)
                vev(f"bwep{j}p{p}")

        def b_xb2sq2(j):
            @block.vector
            def _(v, j=j):
                v = _VSer(v)
                par = j % 2
                p3 = j % 3
                v.tensor_copy(rB[par][:, :], xB[p3][:, :])
                vev(f"bxb2{j}")

            @block.scalar
            def _(sc, j=j):
                par = j % 2
                p3 = j % 3
                wt(sc, f"bwep{j}p3")
                inc(sc.activation(sh[par][:, :], xB[p3][:, :], AF.Square),
                    "ssem", 1, f"bsq2{j}")

        def t_stats2(t, j):
            par = j % 2
            wt(t, f"bxb2{j}", f"bsq2{j}", f"brow{j}")
            for ki in range(NCD):
                t.matmul(PS1[0:1, 0:TB], ones_b[:, 0:1], rB[par][:, ki * TB:(ki + 1) * TB],
                         start=(ki == 0), stop=(ki == NCD - 1), skip_group_check=True)
            li = None
            for ki in range(NCD):
                li = t.matmul(PS1[0:1, TB:2 * TB], ones_b[:, 0:1],
                              sh[par][:, ki * TB:(ki + 1) * TB],
                              start=(ki == 0), stop=(ki == NCD - 1), skip_group_check=True)
            inc(li, "tsem", 1, f"bs2{j}")

        def b_rows2(j):
            @block.vector
            def _(v, j=j):
                v = _VSer(v)
                wt(v, f"bs2{j}", f"bsqr2{j-1}")
                v.tensor_scalar(rowB[0:1, RB_M2:RB_M2 + TB], PS1[0:1, 0:TB], INV_D, None,
                                op0=ALU.mult)
                v.tensor_scalar(rowB[0:1, RB_E2:RB_E2 + TB], PS1[0:1, TB:2 * TB], INV_D, EPS_LN,
                                op0=ALU.mult, op1=ALU.add)
                v.tensor_tensor(rowB[0:1, RB_T1:RB_T1 + TB], rowB[0:1, RB_M2:RB_M2 + TB],
                                rowB[0:1, RB_M2:RB_M2 + TB], op=ALU.mult)
                v.tensor_tensor(rowB[0:1, RB_T2:RB_T2 + TB], rowB[0:1, RB_E2:RB_E2 + TB],
                                rowB[0:1, RB_T1:RB_T1 + TB], op=ALU.subtract)
                v.reciprocal(rowB[0:1, RB_T1:RB_T1 + TB], rowB[0:1, RB_T2:RB_T2 + TB])
                vev(f"brow2{j}")

        def b_sqrt2(j):
            @block.scalar
            def _(sc, j=j):
                wt(sc, f"brow2{j}", f"bbc2{j-1}")
                inc(sc.activation(rowB[0:1, RB_RSTD2:RB_RSTD2 + TB],
                                  rowB[0:1, RB_T1:RB_T1 + TB], AF.Sqrt),
                    "ssem", 1, f"bsqr2{j}")

        def t_bcast2(t, j):
            wt(t, f"bsqr2{j}", f"bevacr{j}")
            t.matmul(PS2[:, 0:TB], ones_f[0:1, :], rowB[0:1, RB_M2:RB_M2 + TB],
                     start=True, stop=True, skip_group_check=True)
            inc(t.matmul(PS2[:, TB:2 * TB], ones_f[0:1, :],
                         rowB[0:1, RB_RSTD2:RB_RSTD2 + TB],
                         start=True, stop=True, skip_group_check=True),
                "tsem", 1, f"bbc2{j}")

        def b_evac2(j):
            @block.vector
            def _(v, j=j):
                v = _VSer(v)
                par = j % 2
                wt(v, f"bbc2{j}")
                v.tensor_copy(m2B_s[:, :], PS2[:, 0:TB])
                v.tensor_copy(rstd2B_s[:, :], PS2[:, TB:2 * TB])
                vev(f"bevac2{j}")

        def b_xn2(j):
            @block.vector
            def _(v, j=j):
                v = _VSer(v)
                par = j % 2
                p3 = j % 3
                wt(v, f"bevac2{j}", f"bs2{j}")
                for c in range(NCD):
                    sl = slice(c * TB, (c + 1) * TB)
                    v.tensor_tensor(wpB[par][:, sl], xB[p3][:, sl],
                                    m2B_s[:, :], op=ALU.subtract)
                for c in range(NCD):
                    sl = slice(c * TB, (c + 1) * TB)
                    v.tensor_tensor(sh[par][:, sl], wpB[par][:, sl],
                                    rstd2B_s[:, :], op=ALU.mult)
                vev(f"bxn2{j}")

        def b_w2g(gidx):
            @block.gpsimd
            def _(g, gidx=gidx):
                gp = gidx % 3
                wt(g, f"gfree{gidx-3}")
                d = g.dma_start(out=w2g[gp][:, :].rearrange("p (f e) -> p f e", f=4),
                                in_=scr_w2_G[gidx % 8])
                d.then_inc(SEM[f"g{gp}"], 16)
                CNT[f"g{gp}"] += 16
                EV[f"w2g{gidx}"] = (f"g{gp}", CNT[f"g{gp}"])

        FB = [PK1, PK2, PB1, PB2]

        def t_f1(t, j, hi):
            if hi == 0:
                wt(t, f"bxn2{j}", f"bfePKa{j-1}")
            if hi == 1:
                wt(t, f"bfePK{j-1}")
            if hi == 2:
                wt(t, f"bfePB{j-1}")
            if j == 0 and hi % 4 == 0:
                wt(t, f"w1c{hi // 4}")
            f1bank = FB[hi % 4]
            wt(t, f"g{j}h{hi-4}")
            par = j % 2
            li = None
            for ki in range(NCD):
                li = t.matmul(f1bank[:, 0:TB],
                              w1_s[:, hi * NCD * 128 + ki * 128: hi * NCD * 128 + ki * 128 + 128],
                              sh[par][:, ki * TB:(ki + 1) * TB],
                              start=(ki == 0), stop=(ki == NCD - 1),
                              skip_group_check=True)
            inc(li, "tsem", 1, f"f1{j}h{hi}")

        def s_gelu(j, hi):
            @block.scalar
            def _(sc, j=j, hi=hi):
                wt(sc, f"f1{j}h{hi}")
                f1bank = FB[hi % 4]
                inc(sc.activation(h_s[:, hi * TB:(hi + 1) * TB], f1bank[:, 0:TB], AF.Gelu,
                                  bias=b1_s[:, hi:hi + 1]), "ssem", 1, f"g{j}h{hi}")

        def t_f2open(t, j):
            wt(t, f"g{j}h{NHC-1}", "consts2", "carry0")
            for mi in range(NCD):
                bank = FB[mi // 2]
                half = mi % 2
                t.matmul(bank[:, half * TB: half * TB + TB],
                         b2b_s[0:1, mi * 128:(mi + 1) * 128], ones_r[0:1, :],
                         start=(half == 0), stop=False, skip_group_check=True)

        def t_f2(t, j, hi):
            gidx = j * 8 + hi // 4
            if hi % 4 == 0:
                wt(t, f"w2g{gidx}")
            gp = gidx % 3
            li = None
            for mi in range(NCD):
                bank = FB[mi // 2]
                half = mi % 2
                li = t.matmul(bank[:, half * TB: half * TB + TB],
                              w2g[gp][:, (hi % 4) * D + mi * 128: (hi % 4) * D + mi * 128 + 128],
                              h_s[:, hi * TB:(hi + 1) * TB],
                              start=False, stop=(hi == NHC - 1),
                              skip_group_check=True)
            inc(li, "tsem", 1, f"f2{j}h{hi}")
            if hi % 4 == 3:
                EV[f"gfree{gidx}"] = EV[f"f2{j}h{hi}"]

        def t_f2last(t, j):
            gidx = j * 8 + 7
            wt(t, f"w2g{gidx}")
            gp = gidx % 3
            li = None
            for mi in range(NCD):
                bank = FB[mi // 2]
                half = mi % 2
                for hi in range(NHC - 4, NHC):
                    li = t.matmul(bank[:, half * TB: half * TB + TB],
                                  w2g[gp][:, (hi % 4) * D + mi * 128: (hi % 4) * D + mi * 128 + 128],
                                  h_s[:, hi * TB:(hi + 1) * TB],
                                  start=False, stop=(hi == NHC - 1),
                                  skip_group_check=True)
                if mi == 1:
                    inc(li, "tsem", 1, f"f2PKa{j}")
                if mi == 3:
                    inc(li, "tsem", 1, f"f2PKb{j}")
            inc(li, "tsem", 1, f"f2{j}h{NHC-1}")
            EV[f"gfree{gidx}"] = EV[f"f2{j}h{NHC-1}"]

        def b_bfePK(j):
            @block.vector
            def _(v, j=j):
                v = _VSer(v)
                p3 = j % 3
                wt(v, f"f2PKa{j}")
                v.tensor_tensor(xB[p3][:, 0:2 * TB], xB[p3][:, 0:2 * TB],
                                PK1[:, 0:2 * TB], op=ALU.add)
                vev(f"bfePKa{j}")
                wt(v, f"f2PKb{j}")
                v.tensor_tensor(xB[p3][:, 2 * TB:4 * TB], xB[p3][:, 2 * TB:4 * TB],
                                PK2[:, 0:2 * TB], op=ALU.add)
                vev(f"bfePK{j}")

        def b_bfePB(j):
            @block.vector
            def _(v, j=j):
                v = _VSer(v)
                p3 = j % 3
                wt(v, f"f2{j}h{NHC-1}")
                v.tensor_tensor(xB[p3][:, 4 * TB:6 * TB], xB[p3][:, 4 * TB:6 * TB],
                                PB1[:, 0:2 * TB], op=ALU.add)
                v.tensor_tensor(xB[p3][:, 6 * TB:8 * TB], xB[p3][:, 6 * TB:8 * TB],
                                PB2[:, 0:2 * TB], op=ALU.add)
                vev(f"bfePB{j}")

        def b_ystore(j):
            @block.sync
            def _(sync, j=j):
                par = j % 2
                p3 = j % 3
                wt(sync, f"bfePB{j}", f"bfePK{j}")
                sy = f"y{par}"
                d = sync.dma_start(out=yout[j], in_=xB[p3][:, :])
                d.then_inc(SEM[sy], 16)
                CNT[sy] += 16
                EV[f"y{j}"] = (sy, CNT[sy])

        LATE_U = []
        b_loads_wu(0)
        b_loads_x(0)
        b_loads_wu(1)

        @block.sync
        def _(sync):
            wt(sync, "ccdone")
            d = sync.dma_start(out=G_s[:, :], in_=cc_out)
            d.then_inc(SEM["gss"], 16)
            CNT["gss"] += 16
            EV["Gs"] = ("gss", CNT["gss"])

        @block.tensor
        def _(t):
            wt(t, "Gs", f"xn{NTA-1}")
            li = None
            for c in range(NCD):
                li = t.matmul(PB1[:, c:c + 1], G_s[:, c * 128:(c + 1) * 128], sel_s[:, :],
                              start=True, stop=True, skip_group_check=True)
            inc(li, "tsem", 1, "sel")

        @block.vector
        def _(v):
            v = _VSer(v)
            wt(v, "sel")
            v.tensor_copy(carryB[:, :], PB1[:, 0:NCD])
            vev("cB")

        b_loads_x(1)
        b_scan(0)

        @block.scalar
        def _(sc):
            LATE_U[0](sc, 1)

        b_scan(1)
        b_sq(0)
        b_sq(1)

        @block.tensor
        def _(t):
            t_brs(t, 0)
        b_rows(0)
        b_sqrt(0)

        @block.tensor
        def _(t):
            t_bbc(t, 0)
        b_evacr(0)
        b_attn(0)
        for mi in range(NCD):
            @block.tensor
            def _(t, mi=mi):
                t_wo(t, 0, mi)
            if mi % 2 == 1:
                b_bwe_pair(0, mi // 2)
        b_xb2sq2(0)

        @block.tensor
        def _(t):
            t_stats2(t, 0)
        b_rows2(0)
        b_sqrt2(0)

        @block.tensor
        def _(t):
            t_bcast2(t, 0)
        b_evac2(0)
        b_xn2(0)

        @block.tensor
        def _(t):
            t_brs(t, 1)
        b_rows(1)
        b_sqrt(1)

        @block.tensor
        def _(t):
            t_bbc(t, 1)
        b_evacr(1)
        b_attn(1)

        for j in range(NTB):
            cj = j + 1
            c2 = j + 2

            if c2 < NTB:
                b_loads_wu(c2)
            b_loads_x(j + 2)

            for hi in range(NHC):
                @block.tensor
                def _(t, j=j, hi=hi):
                    t_f1(t, j, hi)
                s_gelu(j, hi)

                if cj < NTB:
                    if 2 <= hi <= 9:
                        @block.tensor
                        def _(t, cj=cj, mi=hi - 2):
                            t_wo(t, cj, mi)
                        if (hi - 2) % 2 == 1:
                            b_bwe_pair(cj, (hi - 2) // 2)
                    if hi == 10:
                        b_xb2sq2(cj)
                    if hi == 13:
                        @block.tensor
                        def _(t, cj=cj):
                            t_stats2(t, cj)
                        b_rows2(cj)

            @block.tensor
            def _(t, j=j):
                t_f2open(t, j)
            for g in range(8):
                @block.tensor
                def _(t, j=j, g=g):
                    if g == 7:
                        t_f2last(t, j)
                    else:
                        for hi in range(4 * g, 4 * g + 4):
                            t_f2(t, j, hi)
                if g + 3 < 8:
                    b_w2g(j * 8 + g + 3)
                if c2 < NTB:
                    if g == 0:
                        b_scan(c2)
                    if g == 1:
                        b_sq(c2)
                if cj < NTB:
                    if g == 1:
                        b_sqrt2(cj)
                    if g == 2:
                        @block.tensor
                        def _(t, cj=cj):
                            t_bcast2(t, cj)
                        b_evac2(cj)
                        b_xn2(cj)
                if c2 < NTB:
                    if g == 4:
                        @block.tensor
                        def _(t, c2=c2):
                            t_brs(t, c2)
                        b_rows(c2)
                        b_sqrt(c2)
                    if g == 6:
                        @block.tensor
                        def _(t, c2=c2):
                            t_bbc(t, c2)
                        b_evacr(c2)
                        b_attn(c2)

            if j + 1 < NTB:
                for gg in range(3):
                    b_w2g((j + 1) * 8 + gg)

            b_bfePK(j)
            b_bfePB(j)
            b_ystore(j)

        @block.sync
        def _(sync):
            wt(sync, f"y{NTB-1}", f"y{NTB-2}")

        fake = _FakeEng()
        for _name, _fn in PROG:
            _fn(fake)

        PASS[0] = 2
        for _k in CNT:
            CNT[_k] = 0
        with nc.Block() as realblock:
            for _eng_name in ENGS:
                def _run(eng, _eng_name=_eng_name):
                    for _n2, _fn in PROG:
                        if _n2 == _eng_name:
                            _fn(eng)
                getattr(realblock, _eng_name)(_run)

    return nc


def kernel(L0_attn_x, L0_attn_Wr, L0_attn_Wk, L0_attn_Wv, L0_attn_Ww,
           L0_attn_Wo, L0_ffn_w1, L0_ffn_b1, L0_ffn_w2, L0_ffn_b2):
    x = np.ascontiguousarray(np.asarray(L0_attn_x, dtype=np.float32))
    Wr = np.asarray(L0_attn_Wr, dtype=np.float32)
    Wk = np.asarray(L0_attn_Wk, dtype=np.float32)
    Wv = np.asarray(L0_attn_Wv, dtype=np.float32)
    Ww = np.asarray(L0_attn_Ww, dtype=np.float32)
    Wo = np.asarray(L0_attn_Wo, dtype=np.float32)
    w1 = np.ascontiguousarray(np.asarray(L0_ffn_w1, dtype=np.float32))
    b1 = np.asarray(L0_ffn_b1, dtype=np.float32)
    w2 = np.ascontiguousarray(np.asarray(L0_ffn_w2, dtype=np.float32))
    b2 = np.asarray(L0_ffn_b2, dtype=np.float32)

    wrT = np.ascontiguousarray(Wr.T)
    wvT = np.ascontiguousarray(Wv.T)
    wwT = np.ascontiguousarray(Ww.T)
    woT = np.ascontiguousarray(Wo.T)
    wkN = np.ascontiguousarray(Wk)
    w1n = np.ascontiguousarray(
        w1.reshape(NCD, 128, NHC, 128).transpose(1, 2, 0, 3).reshape(128, NHC * NCD * 128))
    b1r = np.ascontiguousarray(b1.reshape(NHC, 128).T)
    b2row = np.ascontiguousarray(b2.reshape(1, D))
    dd = np.arange(D)
    emT = np.ascontiguousarray((dd[:, None] // HD == np.arange(NH)[None, :]).astype(np.float32))
    emask = np.ascontiguousarray(emT.T)
    onesf = np.ones((128, 128), np.float32)

    in_maps = []
    for c in range(NCORES):
        b, h = c // 2, c % 2
        xs = x[b, h * TC:(h + 1) * TC, :]
        xin = np.ascontiguousarray(
            xs.reshape(NTA, TA, NCD, 128).transpose(0, 3, 2, 1).reshape(NTA, 128, NCD * TA))
        sel = np.zeros((2, 1), np.float32)
        if h == 1:
            sel[0, 0] = 1.0
        in_maps.append({
            "xin": xin, "wrT": wrT, "wvT": wvT, "wwT": wwT, "woT": woT,
            "wkN": wkN, "w1n": w1n, "w2": w2, "b1r": b1r, "b2row": b2row,
            "emT": emT, "emask": emask, "onesf": onesf, "selv": sel,
        })

    nc = build_nc()
    res = run_bass_kernel_spmd(nc, in_maps, core_ids=list(range(NCORES)))

    out = np.empty((B, T, D), np.float32)
    for c in range(NCORES):
        b, h = c // 2, c % 2
        yc = np.asarray(res.results[c]["yout"])
        blk = yc.reshape(NTB, 128, NCD, TB).transpose(0, 3, 2, 1).reshape(TC, D)
        out[b, h * TC:(h + 1) * TC, :] = blk
    return out
